# version 12
# baseline (speedup 1.0000x reference)
import numpy as np
import ml_dtypes
from contextlib import ExitStack

import concourse.bass as bass
import concourse.mybir as mybir
from concourse.bass_utils import run_bass_kernel_spmd
from concourse.aot_env import allow_ici_all_to_all

F32 = mybir.dt.float32
BF16 = mybir.dt.bfloat16
AF = mybir.ActivationFunctionType
ALU = mybir.AluOpType
AX = mybir.AxisListType.X

NCORES = 8
B, T, D = 2, 4096, 2048
TPC = 1024
DSL = 256
NST = 16
I_B = 4096
H_B = 64
N_B = 128
CONV_B = 4352
PROJ_B = 8512
CH = 128
NCH = TPC // CH
HF2 = 16384
HF = 8192
EPS_LN = 1e-5
EPS_FFN = float(np.finfo(np.float32).eps)
EPS_B = 1e-6
SCLAMP = -60.0

OFF_GU = 0
OFF_PJ = OFF_GU + DSL * HF2
OFF_WI = OFF_PJ + 1024 * D
OFF_WO = OFF_WI + DSL * PROJ_B
NQ = OFF_WO + 512 * D

DBG = False
PHASES = 5
PREP_LVL = 3


class _StopEmission(Exception):
    pass


def _ap(t, off, dims):
    a = t if isinstance(t, bass.AP) else t.ap()
    return bass.AP(a.tensor, a.offset + off, [list(d) for d in dims])


def _bc(ap2d, nrep):
    return bass.AP(ap2d.tensor, ap2d.offset, [list(ap2d.ap[0]), list(ap2d.ap[1]), [0, nrep]])


class Prog:
    def __init__(self, nc, ctx):
        self.nc, self.ctx = nc, ctx
        self.qs = {k: [] for k in ("sync", "gpsimd", "vector", "scalar", "tensor")}
        self.sems = {}
        self.guards = {}

    def sem(self, name):
        self.sems[name] = [self.ctx.enter_context(self.nc.semaphore(name)), 0]

    def val(self, s):
        return self.sems[s][1]

    def op(self, q, fn, sem=None, k=1):
        if sem is None:
            self.qs[q].append(fn)
            return None
        rec = self.sems[sem]
        rec[1] += k
        h = rec[0]
        self.qs[q].append(lambda E, fn=fn, h=h, k=k: fn(E).then_inc(h, k))
        return rec[1]

    def dma(self, q, out, in_, sem=None):
        return self.op(q, lambda E: E.dma_start(out=out, in_=in_), sem, 16)

    def wait(self, q, sem, v=None):
        rec = self.sems[sem]
        if v is None:
            v = rec[1]
        if v <= 0:
            return
        h = rec[0]
        self.qs[q].append(lambda E, h=h, v=v: E.wait_ge(h, v))

    def gset(self, key, sem):
        self.guards[key] = (sem, self.sems[sem][1])

    def gwait(self, q, key):
        if key in self.guards:
            s, v = self.guards[key]
            self.wait(q, s, v)

    def emit(self, block):
        qs = self.qs

        @block.sync
        def _(E):
            for f in qs["sync"]:
                f(E)

        @block.gpsimd
        def _(E):
            for f in qs["gpsimd"]:
                f(E)

        @block.vector
        def _(E):
            for f in qs["vector"]:
                f(E)

        @block.scalar
        def _(E):
            for f in qs["scalar"]:
                f(E)

        @block.tensor
        def _(E):
            for f in qs["tensor"]:
                f(E)


def build():
    nc = bass.Bass("TRN2", target_bir_lowering=False, detect_race_conditions=False)

    EI = dict(kind="ExternalInput")
    xh = nc.dram_tensor("xh", [TPC + 128, D], F32, **EI)
    guw = nc.dram_tensor("guw", [HF2, DSL], F32, **EI)
    pjw = nc.dram_tensor("pjw", [D, 1024], F32, **EI)
    wint = nc.dram_tensor("wint", [DSL, PROJ_B], F32, **EI)
    wott = nc.dram_tensor("wott", [512, D], F32, **EI)
    xpjt = nc.dram_tensor("xpjt", [D, 160], F32, **EI)
    dtpt = nc.dram_tensor("dtpt", [128, D], F32, **EI)
    cwa = nc.dram_tensor("cwa", [D, 4], F32, **EI)
    cba = nc.dram_tensor("cba", [D, 1], F32, **EI)
    dtba = nc.dram_tensor("dtba", [D, 1], F32, **EI)
    alga = nc.dram_tensor("alga", [DSL, NST], F32, **EI)
    dpar = nc.dram_tensor("dpar", [1, D], F32, **EI)
    cwb = nc.dram_tensor("cwb", [CONV_B, 4], F32, **EI)
    cbb = nc.dram_tensor("cbb", [CONV_B, 1], F32, **EI)
    dtbb = nc.dram_tensor("dtbb", [H_B, 1], F32, **EI)
    algb = nc.dram_tensor("algb", [H_B, 1], F32, **EI)
    dexp = nc.dram_tensor("dexp", [1, I_B], F32, **EI)
    nrmw = nc.dram_tensor("nrmw", [1, I_B], F32, **EI)
    identb = nc.dram_tensor("identb", [128, 128], BF16, **EI)
    identf = nc.dram_tensor("identf", [128, 128], F32, **EI)
    tri01 = nc.dram_tensor("tri01", [128, 128], BF16, **EI)
    selm = nc.dram_tensor("selm", [8, 8], BF16, **EI)
    selb = nc.dram_tensor("selb", [8, 1], F32, **EI)
    hsel = nc.dram_tensor("hsel", [1, 8], F32, **EI)
    epsc = nc.dram_tensor("epsc", [1, 8], F32, **EI)

    out_x = nc.dram_tensor("out", [TPC, D], F32, kind="ExternalOutput")

    KI = dict(kind="ExternalOutput" if DBG else "Internal")
    xres_d = nc.dram_tensor("xres_d", [TPC, D], F32, **KI)
    cc_qin = nc.dram_tensor("cc_qin", [NQ], BF16, kind="Internal")
    cc_qout = nc.dram_tensor("cc_qout", [8 * NQ], BF16, kind="Internal", addr_space="Shared")
    a2a_dI = nc.dram_tensor("a2a_dI", [D, TPC], F32, kind="Internal")
    a2a_dO = nc.dram_tensor("a2a_dO", [D, TPC], F32, kind="Internal")
    a2a_xI = nc.dram_tensor("a2a_xI", [D, TPC], BF16, kind="Internal")
    a2a_xO = nc.dram_tensor("a2a_xO", [D, TPC], BF16, kind="Internal")
    ag_bcI = nc.dram_tensor("ag_bcI", [32, TPC], BF16, kind="Internal")
    ag_bcO = nc.dram_tensor("ag_bcO", [8 * 32, TPC], BF16, kind="Internal", addr_space="Shared")
    a2a_yI = nc.dram_tensor("a2a_yI", [D, TPC], BF16, kind="Internal")
    a2a_yO = nc.dram_tensor("a2a_yO", [D, TPC], BF16, kind="Internal")
    ag_hlI = nc.dram_tensor("ag_hlI", [128, 102], BF16, kind="Internal")
    ag_hlO = nc.dram_tensor("ag_hlO", [8 * 128, 102], BF16, kind="Internal", addr_space="Shared")
    ag_hI = nc.dram_tensor("ag_hI", [N_B, H_B * 64], F32, kind="Internal")
    ag_hO = nc.dram_tensor("ag_hO", [8 * N_B, H_B * 64], F32, kind="Internal", addr_space="Shared")
    ag_sI = nc.dram_tensor("ag_sI", [H_B, 1], F32, kind="Internal")
    ag_sO = nc.dram_tensor("ag_sO", [8 * H_B, 1], F32, kind="Internal", addr_space="Shared")
    sg_tp = nc.dram_tensor("sg_tp", [TPC, I_B], BF16, kind="Internal")
    y3_sp = nc.dram_tensor("y3_sp", [TPC, I_B], BF16, **KI)
    hn_d = nc.dram_tensor("hn_d", [HF, TPC], BF16, kind="Internal")
    sub_d = nc.dram_tensor("sub_d", [TPC, D], BF16, **KI)
    inv_d = nc.dram_tensor("inv_d", [1, TPC], F32, kind="Internal")
    sbc_d = nc.dram_tensor("sbc_d", [H_B, CH], F32, kind="Internal")
    if DBG:
        dbg_qout = nc.dram_tensor("dbg_qout", [8 * NQ], BF16, kind="ExternalOutput")
        dbg_dO = nc.dram_tensor("dbg_dO", [D, TPC], F32, kind="ExternalOutput")
        dbg_xO = nc.dram_tensor("dbg_xO", [D, TPC], BF16, kind="ExternalOutput")
        dbg_bcO = nc.dram_tensor("dbg_bcO", [8 * 32, TPC], BF16, kind="ExternalOutput")
        dbg_yO = nc.dram_tensor("dbg_yO", [D, TPC], BF16, kind="ExternalOutput")
        dbg_t1 = nc.dram_tensor("dbg_t1", [128, 4096], BF16, kind="ExternalOutput")
        dbg_t2 = nc.dram_tensor("dbg_t2", [128, 16384], F32, kind="ExternalOutput")
    send_d = nc.dram_tensor("send_d", [H_B, 1], F32, kind="Internal")
    w8_d = nc.dram_tensor("w8_d", [8, H_B], F32, kind="Internal")

    with ExitStack() as ctx:
        En = ctx.enter_context

        sb_id = En(nc.sbuf_tensor("sb_id", [128, 128], BF16))
        sb_idf = En(nc.sbuf_tensor("sb_idf", [128, 128], F32))
        sb_tri = En(nc.sbuf_tensor("sb_tri", [128, 128], BF16))
        sb_cwa = En(nc.sbuf_tensor("sb_cwa", [128, 64], F32))
        sb_cba = En(nc.sbuf_tensor("sb_cba", [128, 16], F32))
        sb_dtb = En(nc.sbuf_tensor("sb_dtb", [128, 16], F32))
        sb_alg = En(nc.sbuf_tensor("sb_alg", [128, 2 * NST], F32))
        sb_cwb = En(nc.sbuf_tensor("sb_cwb", [128, 136], F32))
        sb_cbb = En(nc.sbuf_tensor("sb_cbb", [128, 34], F32))
        sb_dbb = En(nc.sbuf_tensor("sb_dbb", [H_B, 1], F32))
        sb_anb = En(nc.sbuf_tensor("sb_anb", [H_B, 1], F32))
        sb_dxb = En(nc.sbuf_tensor("sb_dxb", [128, I_B], BF16))
        sb_sel = En(nc.sbuf_tensor("sb_sel", [8, 8], BF16))
        sb_seb = En(nc.sbuf_tensor("sb_seb", [8, 1], F32))
        sb_hse = En(nc.sbuf_tensor("sb_hse", [128, 8], F32))
        sb_eps = En(nc.sbuf_tensor("sb_eps", [128, 8], F32))
        sb_one = En(nc.sbuf_tensor("sb_one", [128, 1], BF16))

        wbuf = En(nc.sbuf_tensor("wbuf", [128, 8192], BF16))
        stg = En(nc.sbuf_tensor("stg", [128, 4096], F32))
        stg2 = En(nc.sbuf_tensor("stg2", [128, 4096], F32))
        big1 = En(nc.sbuf_tensor("big1", [128, 16384], F32))
        big2 = En(nc.sbuf_tensor("big2", [128, 16384], F32))
        bc_b = En(nc.sbuf_tensor("bc_b", [128, 2 * TPC], BF16))
        sb_dt = En(nc.sbuf_tensor("sb_dt", [H_B, TPC], F32))
        sb_sf = En(nc.sbuf_tensor("sb_sf", [H_B, TPC], F32))
        sb_ld = En(nc.sbuf_tensor("sb_ld", [H_B, TPC], F32))
        sb_srt = En(nc.sbuf_tensor("sb_srt", [128, H_B], F32))
        sb_u = En(nc.sbuf_tensor("sb_u", [128, H_B], F32))
        sb_uf = En(nc.sbuf_tensor("sb_uf", [128, H_B], F32))
        sb_ldT = En(nc.sbuf_tensor("sb_ldT", [128, H_B], F32))
        sb_wT = En(nc.sbuf_tensor("sb_wT", [128, H_B], F32))
        sb_bD = En(nc.sbuf_tensor("sb_bD", [128, H_B], F32))
        sb_dto = En(nc.sbuf_tensor("sb_dto", [128, H_B], F32))
        sb_w8 = En(nc.sbuf_tensor("sb_w8", [128, 512], F32))
        sb_hal = En(nc.sbuf_tensor("sb_hal", [128, 102], F32))
        sb_sta = En(nc.sbuf_tensor("sb_sta", [128, 204], BF16))

        ps_a = En(nc.psum_tensor("ps_a", [128, 512], F32))
        ps_b = En(nc.psum_tensor("ps_b", [128, 512], F32))
        ps_c = En(nc.psum_tensor("ps_c", [128, 512], F32))
        ps_d = En(nc.psum_tensor("ps_d", [128, 512], F32))
        ps_e = En(nc.psum_tensor("ps_e", [128, 512], F32))
        ps_f = En(nc.psum_tensor("ps_f", [128, 512], F32))
        ps_t = En(nc.psum_tensor("ps_t", [128, 512], F32))
        ps_u = En(nc.psum_tensor("ps_u", [128, 512], F32))

        p = Prog(nc, ctx)
        for s in ("LD", "LD2", "VEC", "SCL", "TEN", "CCS", "STR", "SPL", "RPL",
                  "LQ", "LS", "LX", "LH", "LWA", "LWB", "RPA", "RPB", "SCQ", "SCC"):
            p.sem(s)
        block = En(nc.Block())

        bf1 = big1[:, :].bitcast(BF16)
        bf2 = big2[:, :].bitcast(BF16)
        bfs = stg[:, :].bitcast(BF16)
        bfs2 = stg2[:, :].bitcast(BF16)

        RG = [list(range(NCORES))]

        def cc(kind, tin, tout):
            return p.op("gpsimd",
                        lambda E: E.collective_compute(kind, ALU.bypass, replica_groups=RG,
                                                       ins=[tin.ap().opt()], outs=[tout.ap().opt()]),
                        "CCS", 1)

        def TT(out, a, b, alu):
            return p.op("vector", lambda E: E.tensor_tensor(out, a, b, alu), "VEC")

        def TS(out, a, s1, alu):
            return p.op("vector", lambda E: E.tensor_single_scalar(out, a, s1, alu), "VEC")

        def TS2(out, a, s1, s2, alu0, alu1):
            return p.op("vector", lambda E: E.tensor_scalar(out, a, s1, s2, alu0, alu1), "VEC")

        def STT(out, a, s1, b, alu0, alu1):
            return p.op("vector", lambda E: E.scalar_tensor_tensor(out, a, s1, b, alu0, alu1), "VEC")

        def VCP(out, in_):
            return p.op("vector", lambda E: E.tensor_copy(out, in_), "VEC")

        def ACT(out, in_, f, bias=0.0, scale=1.0, accum_out=None):
            if accum_out is not None:
                return p.op("scalar", lambda E: E.activation(out, in_, f, bias=bias, scale=scale, accum_out=accum_out), "SCL")
            return p.op("scalar", lambda E: E.activation(out, in_, f, bias=bias, scale=scale), "SCL")

        def SCP(out, in_):
            return p.op("scalar", lambda E: E.copy(out, in_), "SCL")

        def MM(out, lhsT, rhs, st_, sp_, sem=None):
            return p.op("tensor", lambda E: E.matmul(out, lhsT, rhs, start=st_, stop=sp_), sem, 1)

        def VF():
            p.wait("vector", "VEC", p.val("VEC"))

        tr_state = {"b": 0}

        def tr_group(items, fp32=False, evac="scalar"):
            idt = sb_idf if fp32 else sb_id
            for g0 in range(0, len(items), 4):
                grp = items[g0:g0 + 4]
                if fp32:
                    bank = ps_e[:, :]
                    key = "trbank_f"
                else:
                    bank = (ps_t if tr_state["b"] == 0 else ps_u)[:, :].bitcast(BF16)
                    key = ("trbank", tr_state["b"])
                    tr_state["b"] ^= 1
                p.gwait("tensor", key)
                outs = []
                for sl, (dst, srcap) in enumerate(grp):
                    np_ = srcap.ap[0][1]
                    wid = srcap.ap[1][1]
                    outp = bass.AP(bank.tensor, bank.offset + 128 * sl, [[bank.ap[0][0], wid], [1, np_]])
                    p.op("tensor", lambda E, outp=outp, srcap=srcap, idt=idt, np_=np_: E.transpose(
                        outp, srcap, idt[0:np_, 0:np_]), "TEN", 1)
                    outs.append((dst, outp))
                vt = p.val("TEN")
                p.wait(evac, "TEN", vt)
                for dst, outp in outs:
                    if evac == "scalar":
                        SCP(dst, outp)
                    else:
                        VCP(dst, outp)
                p.gset(key, "SCL" if evac == "scalar" else "VEC")

        def tr_evac(dst, src, fp32=False, evac="scalar"):
            tr_group([(dst, src)], fp32=fp32, evac=evac)

        p.dma("sync", sb_id[:, :], identb.ap(), "LD")
        p.dma("sync", sb_idf[:, :], identf.ap(), "LD")
        p.dma("sync", sb_tri[:, :], tri01.ap(), "LD")
        for k in range(16):
            p.dma("sync", sb_cwa[:, 4 * k:4 * k + 4], cwa[128 * k:128 * (k + 1), :], "LD")
            p.dma("sync", sb_cba[:, k:k + 1], cba[128 * k:128 * (k + 1), :], "LD")
            p.dma("sync", sb_dtb[:, k:k + 1], dtba[128 * k:128 * (k + 1), :], "LD")
        p.dma("sync", sb_alg[:, 0:NST], alga[0:128, :], "LD")
        p.dma("sync", sb_alg[:, NST:2 * NST], alga[128:256, :], "LD")
        for k in range(34):
            p.dma("sync", sb_cwb[:, 4 * k:4 * k + 4], cwb[128 * k:128 * (k + 1), :], "LD")
            p.dma("sync", sb_cbb[:, k:k + 1], cbb[128 * k:128 * (k + 1), :], "LD")
        p.dma("sync", sb_dbb[:, :], dtbb.ap(), "LD")
        p.dma("sync", sb_anb[:, :], algb.ap(), "LD")
        p.dma("gpsimd", sb_dxb[:, :], _ap(dexp, 0, [[0, 128], [1, I_B]]), "LD2")
        p.dma("sync", sb_sel[:, :], selm.ap(), "LD")
        p.dma("sync", sb_seb[:, :], selb.ap(), "LD")
        p.dma("sync", sb_hse[:, :], _ap(hsel, 0, [[0, 128], [1, 8]]), "LD")
        p.dma("sync", sb_eps[:, :], _ap(epsc, 0, [[0, 128], [1, 8]]), "LD")
        for k in range(16):
            p.dma("gpsimd", wbuf[:, 160 * k:160 * (k + 1)], xpjt[128 * k:128 * (k + 1), :], "LD2")
        p.dma("gpsimd", wbuf[:, 2560:4608], dtpt[:, :], "LD2")
        v_small = p.val("LD")
        v_small2 = p.val("LD2")

        p.wait("scalar", "LD", v_small)
        ACT(stg[:, 0:2 * NST], sb_alg[:, :], AF.Exp)
        ACT(stg[0:H_B, 40:41], sb_anb[:, :], AF.Exp)
        v0 = p.val("SCL")
        p.wait("vector", "SCL", v0)
        TS(sb_alg[:, :], stg[:, 0:2 * NST], -1.0, ALU.mult)
        TS(sb_anb[:, :], stg[0:H_B, 40:41], -1.0, ALU.mult)
        p.op("vector", lambda E: E.memset(sb_one[:, :], 1.0), "VEC")

        def quant_ops(width, srccol=0, freekey=None):
            ngr = width // 64
            p.op("vector", lambda E, width=width, srccol=srccol: E.tensor_copy(
                bfs[:, 0:width], stg2[:, srccol:srccol + width]), "VEC")
            if freekey is not None:
                p.gset(freekey, "VEC")
            wb3 = bfs[:, 0:width].rearrange("q (g k) -> q g k", k=64)
            p.op("vector", lambda E, wb3=wb3, ngr=ngr: E.tensor_reduce(
                stg[:, 3000:3000 + ngr], wb3, AX, ALU.add, apply_absolute_value=True), "VEC")
            VF()
            p.op("vector", lambda E, ngr=ngr: E.tensor_single_scalar(
                bfs[:, 7936:7936 + ngr], stg[:, 3000:3000 + ngr], 1.0 / 64.0, ALU.mult), "VEC")
            VF()
            p.op("vector", lambda E, ngr=ngr: E.tensor_copy(stg[:, 3100:3100 + ngr], bfs[:, 7936:7936 + ngr]), "VEC")
            VF()
            p.op("vector", lambda E, ngr=ngr: E.tensor_single_scalar(
                stg[:, 3100:3100 + ngr], stg[:, 3100:3100 + ngr], 1e-8, ALU.max), "VEC")
            VF()
            p.op("vector", lambda E, ngr=ngr: E.reciprocal(stg[:, 3200:3200 + ngr], stg[:, 3100:3100 + ngr]), "VEC")
            VF()
            p.op("vector", lambda E, ngr=ngr: E.tensor_single_scalar(
                stg[:, 3300:3300 + ngr], stg[:, 3100:3100 + ngr], -1.0, ALU.mult), "VEC")
            VF()
            invb = _bc(stg[:, 3200:3200 + ngr], 64)
            ub = bfs[:, 2048:2048 + width].rearrange("q (g k) -> q g k", k=64)
            p.op("vector", lambda E, wb3=wb3, invb=invb, ub=ub: E.tensor_tensor(ub, wb3, invb, ALU.mult), "VEC")
            p.op("vector", lambda E, width=width: E.tensor_single_scalar(
                bfs[:, 4096:4096 + width], bfs[:, 2048:2048 + width], 0.5, ALU.is_gt), "VEC")
            p.op("vector", lambda E, width=width: E.scalar_tensor_tensor(
                bfs[:, 4096:4096 + width], bfs[:, 2048:2048 + width], -0.5, bfs[:, 4096:4096 + width], ALU.is_lt, ALU.subtract), "VEC")
            negs = _bc(stg[:, 3300:3300 + ngr], 64)
            wh = bfs[:, 0:width].rearrange("q (g k) -> q g k", k=64)
            qb = bfs[:, 4096:4096 + width].rearrange("q (g k) -> q g k", k=64)
            return p.op("vector", lambda E, wh=wh, qb=qb, negs=negs: E.tensor_tensor(wh, qb, negs, ALU.mult), "VEC")

        for it in range(32 if PREP_LVL >= 1.2 else 0):
            q = it % 2
            src = _ap(guw, it * 512 * DSL, [[DSL, 128], [128 * DSL, 4], [1, DSL]])
            p.gwait("sync", ("q_stg2", q))
            ldv = p.dma("sync", stg2[:, q * 1024: q * 1024 + 1024], src, "LQ")
            p.wait("vector", "LQ", ldv)
            vq = quant_ops(1024, srccol=q * 1024, freekey=("q_stg2", q))
            p.wait("tensor", "VEC", vq)
            p.wait("tensor", "LD", v_small)
            for db in range(2):
                reg = bfs2[:, 4096 + 512 * db: 4608 + 512 * db]
                for tt in range(4):
                    tr_evac(reg[:, 128 * tt:128 * (tt + 1)] if False else
                            bfs2[:, 4096 + 512 * db + 128 * tt: 4224 + 512 * db + 128 * tt],
                            bfs[:, tt * 256 + db * 128: tt * 256 + db * 128 + 128])
                vsc = p.val("SCL")
                p.wait("sync", "SCL", vsc)
                dst = _ap(cc_qin, OFF_GU + (db * 128) * HF2 + it * 512, [[HF2, 128], [1, 512]])
                p.dma("sync", dst, bfs2[:, 4096 + 512 * db: 4608 + 512 * db], "SCQ")
            p.gset("q_bfs2w", "SCQ")
            p.gwait("vector", "q_bfs2w")

        for it in range(16 if PREP_LVL >= 1.4 else 0):
            q = it % 2
            p.gwait("sync", ("q_stg2", q))
            ldv = p.dma("sync", stg2[:, q * 1024: q * 1024 + 1024], pjw[128 * it:128 * (it + 1), :], "LQ")
            p.wait("vector", "LQ", ldv)
            vq = quant_ops(1024, srccol=q * 1024, freekey=("q_stg2", q))
            p.wait("tensor", "VEC", vq)
            for hb in range(8):
                tr_evac(bfs2[:, 4096 + 128 * (hb % 8): 4224 + 128 * (hb % 8)],
                        bfs[:, hb * 128:(hb + 1) * 128])
                vsc = p.val("SCL")
                p.wait("sync", "SCL", vsc)
                dst = _ap(cc_qin, OFF_PJ + (hb * 128) * D + it * 128, [[D, 128], [1, 128]])
                p.dma("sync", dst, bfs2[:, 4096 + 128 * (hb % 8): 4224 + 128 * (hb % 8)], "SCQ")
            p.gset("q_bfs2w", "SCQ")
            p.gwait("vector", "q_bfs2w")
            p.gwait("scalar", "q_bfs2w")

        p.wait("sync", "VEC", p.val("VEC"))
        for rt in range(2 if PREP_LVL >= 1.6 else 0):
            p.dma("sync", stg[:, 0:4096], wint[128 * rt:128 * (rt + 1), 0:4096], "LQ")
            p.dma("sync", stg2[:, 0:4096], wint[128 * rt:128 * (rt + 1), 4096:8192], "LQ")
            ldv = p.dma("sync", big1[:, 0:320], wint[128 * rt:128 * (rt + 1), 8192:8512], "LQ")
            p.wait("scalar", "LQ", ldv)
            SCP(bf2[:, 0:4096], stg[:, 0:4096])
            SCP(bf2[:, 4096:8192], stg2[:, 0:4096])
            SCP(bf2[:, 8192:8512], big1[:, 0:320])
            vsc = p.val("SCL")
            p.wait("sync", "SCL", vsc)
            dst = _ap(cc_qin, OFF_WI + (rt * 128) * PROJ_B, [[PROJ_B, 128], [1, PROJ_B]])
            vstr = p.dma("sync", dst, bf2[:, 0:8512], "SCQ")
            p.wait("scalar", "SCQ", vstr)
        for rt in range(4 if PREP_LVL >= 1.8 else 0):
            ldv = p.dma("sync", stg[:, 0:2048], wott[128 * rt:128 * (rt + 1), :], "LQ")
            p.wait("scalar", "LQ", ldv)
            SCP(bf2[:, 0:2048], stg[:, 0:2048])
            vsc = p.val("SCL")
            p.wait("sync", "SCL", vsc)
            dst = _ap(cc_qin, OFF_WO + (rt * 128) * D, [[D, 128], [1, D]])
            vstr = p.dma("sync", dst, bf2[:, 0:2048], "SCQ")
            p.wait("scalar", "SCQ", vstr)

        p.wait("gpsimd", "SCQ", p.val("SCQ"))
        if PREP_LVL >= 3:
            v_qag = cc("AllGather", cc_qin, cc_qout)
        else:
            v_qag = 0

        def ln_full(x_ap, out_bf, mcol, vcol, junk):
            ACT(junk, x_ap, AF.Square, accum_out=vcol)
            ACT(junk, x_ap, AF.Identity, accum_out=mcol)
            p.wait("vector", "SCL", p.val("SCL"))
            TS(mcol, mcol, 1.0 / D, ALU.mult)
            TS(vcol, vcol, 1.0 / D, ALU.mult)
            VF()
            STT(vcol, mcol, mcol, vcol, ALU.mult, ALU.subtract)
            VF()
            TS(vcol, vcol, -1.0, ALU.mult)
            p.wait("scalar", "VEC", p.val("VEC"))
            ACT(vcol, vcol, AF.Sqrt, bias=sb_eps[:, 0:1])
            p.wait("vector", "SCL", p.val("SCL"))
            p.op("vector", lambda E: E.reciprocal(vcol, vcol), "VEC")
            VF()
            p.op("vector", lambda E: E.tensor_scalar(out_bf, x_ap, mcol, vcol, ALU.subtract, ALU.mult), "VEC")
            return p.val("VEC")

        MC = stg[:, 4094:4095]
        VC = stg[:, 4095:4096]
        MCs = (stg[:, 4090:4091], stg[:, 4092:4093])
        VCs = (stg[:, 4091:4092], stg[:, 4093:4094])
        JKB = big2[:, 8192:10240]

        def ln_transpose_pass(src_dram, dstT_bf, W, ntiles, colbase):
            for tt in range(ntiles):
                q = tt % 2
                p.gwait("sync", ("lnp_stg2", q))
                p.gwait("sync", "lnp_spill")
                ldv = p.dma("sync", stg2[:, q * 2048: q * 2048 + 2048], src_dram[128 * tt:128 * (tt + 1), :], "LX")
                p.wait("scalar", "LX", ldv)
                p.gwait("vector", ("lnp_bfs", q))
                vln = ln_full(stg2[:, q * 2048: q * 2048 + 2048], bfs[:, q * 2048: q * 2048 + 2048],
                              MCs[q], VCs[q], JKB)
                p.gset(("lnp_stg2", q), "VEC")
                yield tt, vln
                p.wait("tensor", "VEC", vln)
                for dbk in range(16):
                    tr_evac(dstT_bf[:, dbk * W + colbase + tt * 128: dbk * W + colbase + tt * 128 + 128],
                            bfs[:, q * 2048 + 128 * dbk: q * 2048 + 128 * (dbk + 1)], evac="vector")
                p.gset(("lnp_bfs", q), "VEC")

        wrote_out = [False]

        def ckpt(k):
            if PHASES < k:
                raise _StopEmission()

        try:
            ckpt(2)
            XL = 1152
            p.wait("sync", "VEC", p.val("VEC"))
            p.wait("sync", "SCL", p.val("SCL"))
            for tt, vln in ln_transpose_pass(xh, bf1, XL, 9, 0):
                if tt >= 1:
                    qs = tt % 2
                    p.wait("sync", "VEC", vln)
                    p.dma("sync", xres_d[128 * (tt - 1):128 * tt, :], stg2[:, qs * 2048: qs * 2048 + 2048], "SPL")
                    p.gset("lnp_spill", "SPL")
            v_xlnT = p.val("SCL")

            p.wait("vector", "SCL", v_xlnT)
            p.wait("vector", "LD", v_small)
            for dbk in range(16):
                base = dbk * XL
                p.gwait("vector", "p1_conv")
                p.op("vector", lambda E, base=base, dbk=dbk: E.tensor_scalar(
                    stg[:, 0:1024], bf1[:, base + 125: base + 1149], sb_cwa[:, 4 * dbk:4 * dbk + 1], None, ALU.mult), "VEC")
                for k2 in range(1, 4):
                    STT(stg[:, 0:1024], bf1[:, base + 125 + k2: base + 1149 + k2],
                        sb_cwa[:, 4 * dbk + k2:4 * dbk + k2 + 1], stg[:, 0:1024], ALU.mult, ALU.add)
                vv = p.val("VEC")
                p.wait("scalar", "VEC", vv)
                ACT(bf2[:, dbk * 1024:(dbk + 1) * 1024], stg[:, 0:1024], AF.Silu, bias=sb_cba[:, dbk:dbk + 1])
                p.gset("p1_conv", "SCL")
            v_xcT = p.val("SCL")

            p.wait("tensor", "SCL", v_xcT)
            p.wait("tensor", "LD2", v_small2)
            for ch_ in range(2):
                for dbk in range(16):
                    MM(ps_a[:, :], wbuf[:, 160 * dbk:160 * dbk + 128],
                       bf2[:, dbk * 1024 + ch_ * 512: dbk * 1024 + ch_ * 512 + 512], dbk == 0, dbk == 15)
                for dbk in range(16):
                    MM(ps_b[0:32, :], wbuf[:, 160 * dbk + 128:160 * dbk + 160],
                       bf2[:, dbk * 1024 + ch_ * 512: dbk * 1024 + ch_ * 512 + 512], dbk == 0, dbk == 15,
                       sem=("TEN" if dbk == 15 else None))
                vt = p.val("TEN")
                p.wait("scalar", "TEN", vt)
                SCP(bfs2[:, ch_ * 512: ch_ * 512 + 512], ps_a[:, :])
                SCP(bfs2[0:32, 1024 + ch_ * 512: 1536 + ch_ * 512], ps_b[0:32, :])
                vsc = p.val("SCL")
                p.gset("p1_psab", "SCL")
                p.wait("sync", "SCL", vsc)
                p.dma("sync", ag_bcI[:, ch_ * 512:(ch_ + 1) * 512], bfs2[0:32, 1024 + ch_ * 512: 1536 + ch_ * 512], "SCC")
                if ch_ == 0:
                    p.gwait("tensor", "p1_psab")

            p.wait("tensor", "SCL", p.val("SCL"))
            for dbk in range(16):
                for ch_ in range(2):
                    pss = ps_c if ch_ == 0 else ps_d
                    p.gwait("tensor", ("p1_del", ch_))
                    MM(pss[:, :], wbuf[:, 2560 + 128 * dbk:2560 + 128 * (dbk + 1)],
                       bfs2[:, ch_ * 512: ch_ * 512 + 512], True, True, sem="TEN")
                    vt = p.val("TEN")
                    p.wait("scalar", "TEN", vt)
                    ACT(stg[:, 1024 + ch_ * 512: 1536 + ch_ * 512], pss[:, :], AF.Abs, bias=sb_dtb[:, dbk:dbk + 1])
                    ACT(stg[:, 2048 + ch_ * 512: 2560 + ch_ * 512], pss[:, :], AF.Relu, bias=sb_dtb[:, dbk:dbk + 1])
                    vsc0 = p.val("SCL")
                    p.wait("scalar", "SCL", vsc0)
                    ACT(stg[:, 1024 + ch_ * 512: 1536 + ch_ * 512], stg[:, 1024 + ch_ * 512: 1536 + ch_ * 512], AF.Exp, scale=-1.0)
                    vsc1 = p.val("SCL")
                    p.wait("scalar", "SCL", vsc1)
                    ACT(stg[:, 1024 + ch_ * 512: 1536 + ch_ * 512], stg[:, 1024 + ch_ * 512: 1536 + ch_ * 512], AF.Ln, bias=1.0)
                    vsc2 = p.val("SCL")
                    p.wait("vector", "SCL", vsc2)
                    TT(stg[:, ch_ * 512: ch_ * 512 + 512], stg[:, 1024 + ch_ * 512: 1536 + ch_ * 512],
                       stg[:, 2048 + ch_ * 512: 2560 + ch_ * 512], ALU.add)
                    p.gset(("p1_del", ch_), "VEC")
                p.gwait("vector", "p1_dxc")
                TT(bfs[:, 2048:3072], stg[:, 0:1024], bf2[:, dbk * 1024:(dbk + 1) * 1024], ALU.mult)
                vv = p.val("VEC")
                p.wait("sync", "VEC", vv)
                p.dma("sync", a2a_dI[128 * dbk:128 * (dbk + 1), :], stg[:, 0:1024], "SCC")
                p.dma("sync", a2a_xI[128 * dbk:128 * (dbk + 1), :], bfs[:, 2048:3072], "SCC")
                p.gset("p1_dxc", "SCC")
                p.gwait("scalar", "p1_dxc")

            p.wait("gpsimd", "SCC", p.val("SCC"))
            cc("AllToAll", a2a_dI, a2a_dO)
            v_a2ax = cc("AllToAll", a2a_xI, a2a_xO)
            v_agbc = cc("AllGather", ag_bcI, ag_bcO)

            p.wait("sync", "CCS", v_a2ax)
            for half in range(2):
                hb0 = 4 * half
                if half == 1:
                    p.wait("sync", "SCL", p.val("SCL"))
                    p.wait("sync", "VEC", p.val("VEC"))
                for dt_ in range(2):
                    src = _ap(a2a_dO, dt_ * 128 * TPC + hb0 * DSL * TPC, [[TPC, 128], [DSL * TPC, 4], [1, TPC]])
                    p.dma("sync", big1[:, dt_ * 4096: dt_ * 4096 + 4096], src, "LS")
                    srcx = _ap(a2a_xO, dt_ * 128 * TPC + hb0 * DSL * TPC, [[TPC, 128], [DSL * TPC, 4], [1, TPC]])
                    p.dma("sync", bf1[:, 16384 + dt_ * 4096: 20480 + dt_ * 4096], srcx, "LS")
                v_ldh = p.val("LS")
                p.wait("vector", "LS", v_ldh)
                p.op("vector", lambda E: E.memset(big2[:, 8192:16384], 0.0), "VEC")
                p.wait("sync", "CCS", v_agbc)
                for n_ in range(NST):
                    slot = n_ % 2
                    rsem = "RPA" if slot == 0 else "RPB"
                    slB = wbuf[:, 0:4096] if slot == 0 else bf1[:, 24576:28672]
                    slC = wbuf[:, 4096:8192] if slot == 0 else bf1[:, 28672:32768]
                    p.gwait("sync", ("scan_w", slot))
                    srcB = _ap(ag_bcO, hb0 * 32 * TPC + n_ * TPC, [[0, 128], [32 * TPC, 4], [1, TPC]])
                    vB = p.dma("sync", slB, srcB, rsem)
                    srcC = _ap(ag_bcO, hb0 * 32 * TPC + (NST + n_) * TPC, [[0, 128], [32 * TPC, 4], [1, TPC]])
                    vC = p.dma("sync", slC, srcC, rsem)
                    for dt_ in range(2):
                        p.wait("scalar", "LS", v_ldh)
                        p.gwait("scalar", "scan_a")
                        ACT(big2[:, 0:4096], big1[:, dt_ * 4096: dt_ * 4096 + 4096], AF.Exp,
                            scale=sb_alg[:, dt_ * NST + n_: dt_ * NST + n_ + 1])
                        vsa = p.val("SCL")
                        p.wait("vector", rsem, vB)
                        TT(bfs[:, 0:4096], bf1[:, 16384 + dt_ * 4096: 20480 + dt_ * 4096], slB, ALU.mult)
                        p.wait("vector", "SCL", vsa)
                        p.op("vector", lambda E: E.tensor_tensor_scan(
                            big2[:, 4096:8192], big2[:, 0:4096], bfs[:, 0:4096], 0.0, ALU.mult, ALU.add), "VEC")
                        p.gset("scan_a", "VEC")
                        p.wait("vector", rsem, vC)
                        TT(stg[:, 0:4096], big2[:, 4096:8192], slC, ALU.mult)
                        TT(big2[:, 8192 + dt_ * 4096: 12288 + dt_ * 4096],
                           big2[:, 8192 + dt_ * 4096: 12288 + dt_ * 4096], stg[:, 0:4096], ALU.add)
                    p.gset(("scan_w", slot), "VEC")
                VCP(bfs[:, 0:4096], big2[:, 8192:12288])
                VCP(bfs2[:, 0:4096], big2[:, 12288:16384])
                vv = p.val("VEC")
                p.wait("sync", "VEC", vv)
                for dt_ in range(2):
                    dst = _ap(a2a_yI, dt_ * 128 * TPC + hb0 * DSL * TPC, [[TPC, 128], [DSL * TPC, 4], [1, TPC]])
                    p.dma("sync", dst, (bfs if dt_ == 0 else bfs2)[:, 0:4096], "SCC")
                p.gset("scan_y", "SCC")
                if half == 0:
                    p.gwait("vector", "scan_y")
                    p.gwait("sync", "scan_y")

            if DBG:
                p.wait("sync", "VEC", p.val("VEC"))
                p.wait("sync", "SCL", p.val("SCL"))
                p.dma("sync", _ap(dbg_t1, 0, [[4096, 128], [1, 4096]]), bfs[:, 0:4096], "STR")
                p.dma("sync", _ap(dbg_t2, 0, [[16384, 128], [1, 8192]]), big2[:, 0:8192], "STR")
            p.wait("gpsimd", "SCC", p.val("SCC"))
            v_a2ay = cc("AllToAll", a2a_yI, a2a_yO)
            p.wait("sync", "CCS", v_a2ay)

            ldd = p.dma("sync", big2[:, 6144:8192], _ap(dpar, 0, [[0, 128], [1, D]]), "LX")
            p.wait("sync", "SPL", p.val("SPL"))
            for tt in range(8):
                p.gwait("sync", "ap_tile")
                p.dma("sync", stg2[:, 0:2048], xres_d[128 * tt:128 * (tt + 1), :], "LX")
                for dbk in range(16):
                    j, r = (dbk * 128) // DSL, (dbk * 128) % DSL
                    src = _ap(a2a_yO, j * DSL * TPC + r * TPC + tt * 128, [[TPC, 128], [1, 128]])
                    p.dma("sync", bfs[:, 2048 + 128 * dbk: 2176 + 128 * dbk], src, "LX")
                ldv = p.val("LX")
                p.wait("tensor", "LX", ldv)
                p.gwait("tensor", "ap_yT")
                for dbk in range(16):
                    tr_evac(bfs[:, 128 * dbk:128 * (dbk + 1)], bfs[:, 2048 + 128 * dbk: 2176 + 128 * dbk],
                            evac="vector")
                p.gset("ap_yT", "VEC")
                p.wait("scalar", "LX", ldv)
                vln = ln_full(stg2[:, 0:2048], bf2[:, 4096:6144], MC, VC, big2[:, 8192:10240])
                p.wait("vector", "LX", ldv)
                TT(stg2[:, 0:2048], stg2[:, 0:2048], bfs[:, 0:2048], ALU.add)
                TT(big2[:, 10240:12288], bf2[:, 4096:6144], big2[:, 6144:8192], ALU.mult)
                TT(stg2[:, 0:2048], stg2[:, 0:2048], big2[:, 10240:12288], ALU.add)
                vv = p.val("VEC")
                p.wait("sync", "VEC", vv)
                p.dma("sync", xres_d[128 * tt:128 * (tt + 1), :], stg2[:, 0:2048], "SPL")
                p.gset("ap_tile", "SPL")

            ckpt(3)

            def emit_ffn(ph):
                p.wait("sync", "SPL", p.val("SPL"))
                for tt, vln in ln_transpose_pass(xres_d, bf1, 1024, 8, 0):
                    pass
                v_x2 = p.val("SCL")
                p.wait("tensor", "SCL", v_x2)
                p.wait("tensor", "VEC", p.val("VEC"))
                p.wait("sync", "CCS", v_qag)

                GWSL = [(wbuf[:, 0:8192], bf2[:, 0:8192]),
                        (bf2[:, 8192:16384], bf1[:, 16384:24576])]

                pend_sq = [None]

                def flush_sq():
                    if pend_sq[0] is None:
                        return
                    vsq_, hp_ = pend_sq[0]
                    pend_sq[0] = None
                    p.wait("tensor", "SCL", vsq_)
                    MM(ps_e[0:1, 0:512], sb_one[:, :], bfs2[:, 0:512], hp_ == 0, hp_ == 63)
                    MM(ps_f[0:1, 0:512], sb_one[:, :], bfs2[:, 512:1024], hp_ == 0, hp_ == 63,
                       sem=("TEN" if hp_ == 63 else None))
                    p.gset("ffn_hsq", "TEN")

                for hp4 in range(16):
                    slot = hp4 % 2
                    wsem = "LWA" if slot == 0 else "LWB"
                    gslot, uslot = GWSL[slot]
                    p.gwait("sync", ("ffn_w", slot))
                    for j in range(8):
                        srcg = _ap(cc_qout, j * NQ + OFF_GU + hp4 * 512, [[HF2, 128], [128 * HF2, 2], [1, 512]])
                        p.dma("sync", gslot[:, j * 1024: j * 1024 + 1024], srcg, wsem)
                        srcu = _ap(cc_qout, j * NQ + OFF_GU + HF + hp4 * 512, [[HF2, 128], [128 * HF2, 2], [1, 512]])
                        p.dma("sync", uslot[:, j * 1024: j * 1024 + 1024], srcu, wsem)
                    ldv = p.val(wsem)
                    p.wait("tensor", wsem, ldv)
                    for hpi in range(4):
                        hp = hp4 * 4 + hpi
                        gpar = hp % 2
                        pga, pgb = (ps_a, ps_b) if gpar == 0 else (ps_t, ps_u)
                        flush_sq()
                        p.gwait("tensor", ("ffn_psab", gpar))
                        for ch_ in range(2):
                            for dbk in range(16):
                                MM(pga[:, :] if ch_ == 0 else pgb[:, :],
                                   gslot[:, dbk * 512 + hpi * 128: dbk * 512 + hpi * 128 + 128],
                                   bf1[:, dbk * 1024 + ch_ * 512: dbk * 1024 + ch_ * 512 + 512], dbk == 0, dbk == 15,
                                   sem=("TEN" if (ch_ == 1 and dbk == 15) else None))
                        vt = p.val("TEN")
                        p.wait("scalar", "TEN", vt)
                        p.gwait("scalar", "ffn_sg")
                        ACT(bfs[:, 0:512], pga[:, :], AF.Silu)
                        ACT(bfs[:, 512:1024], pgb[:, :], AF.Silu)
                        vsg = p.val("SCL")
                        p.gset(("ffn_psab", gpar), "SCL")
                        p.gwait("tensor", "ffn_pscd")
                        for ch_ in range(2):
                            for dbk in range(16):
                                MM(ps_c[:, :] if ch_ == 0 else ps_d[:, :],
                                   uslot[:, dbk * 512 + hpi * 128: dbk * 512 + hpi * 128 + 128],
                                   bf1[:, dbk * 1024 + ch_ * 512: dbk * 1024 + ch_ * 512 + 512], dbk == 0, dbk == 15,
                                   sem=("TEN" if (ch_ == 1 and dbk == 15) else None))
                        if hpi == 3:
                            p.gset(("ffn_w", slot), "TEN")
                        vt2 = p.val("TEN")
                        p.wait("vector", "TEN", vt2)
                        p.wait("vector", "SCL", vsg)
                        p.gwait("vector", "ffn_h")
                        p.gwait("vector", "ffn_hq2")
                        TT(bfs[:, 1024:1536], bfs[:, 0:512], ps_c[:, :], ALU.mult)
                        TT(bfs[:, 1536:2048], bfs[:, 512:1024], ps_d[:, :], ALU.mult)
                        vh = p.val("VEC")
                        p.gset("ffn_sg", "VEC")
                        p.gset("ffn_pscd", "VEC")
                        p.wait("scalar", "VEC", vh)
                        p.gwait("scalar", "ffn_hsq")
                        ACT(bfs2[:, 0:1024], bfs[:, 1024:2048], AF.Square)
                        p.gset("ffn_hq2", "SCL")
                        vsq = p.val("SCL")
                        pend_sq[0] = (vsq, hp)
                        p.wait("sync", "VEC", vh)
                        p.dma("sync", hn_d[128 * hp:128 * (hp + 1), :], bfs[:, 1024:2048], "SPL")
                        p.gset("ffn_h", "SPL")
                flush_sq()
                v_ss = p.val("TEN")
                p.wait("scalar", "TEN", v_ss)
                ACT(stg[0:1, 0:512], ps_e[0:1, 0:512], AF.Sqrt, bias=sb_eps[0:1, 1:2], scale=sb_eps[0:1, 3:4])
                ACT(stg[0:1, 512:1024], ps_f[0:1, :], AF.Sqrt, bias=sb_eps[0:1, 1:2], scale=sb_eps[0:1, 3:4])
                vsc = p.val("SCL")
                p.wait("vector", "SCL", vsc)
                p.op("vector", lambda E: E.reciprocal(stg[0:1, 0:1024], stg[0:1, 0:1024]), "VEC")
                vv = p.val("VEC")
                p.wait("sync", "VEC", vv)
                p.dma("sync", inv_d[:, :], stg[0:1, 0:1024], "STR")
                p.wait("sync", "STR", p.val("STR"))
                vrp = p.dma("sync", stg2[:, 0:1024], _ap(inv_d, 0, [[0, 128], [1, 1024]]), "RPA")

                p.wait("sync", "SPL", p.val("SPL"))
                PBK = (ps_a, ps_b, ps_c, ps_d)
                for ch_ in range(2):
                    p.gwait("sync", "ffn_hnres")
                    for kq in range(16):
                        src = _ap(hn_d, (kq * 512) * TPC + ch_ * 512, [[TPC, 128], [128 * TPC, 4], [1, 512]])
                        p.dma("sync", bf2[:, kq * 2048: (kq + 1) * 2048], src, "LH")
                    v_hn = p.val("LH")
                    for mg in range(4):
                        for k in range(64):
                            sl = (k // 8) % 2
                            wsem2 = "LWA" if sl == 0 else "LWB"
                            if k % 8 == 0:
                                j = k // 8
                                p.gwait("sync", ("ffn_pw", sl))
                                srcw = _ap(cc_qout, j * NQ + OFF_PJ + mg * 512, [[D, 128], [128 * D, 8], [1, 512]])
                                ldv = p.dma("sync", wbuf[:, sl * 4096: sl * 4096 + 4096], srcw, wsem2)
                                p.wait("tensor", wsem2, ldv)
                            wap = wbuf[:, sl * 4096 + (k % 8) * 512: sl * 4096 + (k % 8) * 512 + 512]
                            if k == 0:
                                p.wait("tensor", "LH", v_hn)
                                for mi in range(4):
                                    p.gwait("tensor", ("ffn_pp", mi))
                            for mi in range(4):
                                MM(PBK[mi][:, :], wap[:, mi * 128: mi * 128 + 128], bf2[:, k * 512: k * 512 + 512],
                                   k == 0, k == 63, sem=("TEN" if mi == 3 else None))
                            if k % 8 == 7:
                                p.gset(("ffn_pw", sl), "TEN")
                        vt = p.val("TEN")
                        p.wait("vector", "TEN", vt)
                        p.wait("vector", "RPA", vrp)
                        for mi in range(4):
                            p.gwait("vector", "ffn_pe")
                            TT(bfs[:, 1024:1536], PBK[mi][:, :], stg2[:, ch_ * 512: ch_ * 512 + 512], ALU.mult)
                            vv = p.val("VEC")
                            p.gset(("ffn_pp", mi), "VEC")
                            p.wait("tensor", "VEC", vv)
                            if mi == 0:
                                p.gwait("scalar", "ffn_pe2")
                            for sub in range(4):
                                tr_evac(bfs2[:, 2048 + 512 * sub + 128 * mi: 2176 + 512 * sub + 128 * mi],
                                        bfs[:, 1024 + 128 * sub: 1152 + 128 * sub])
                            p.gset("ffn_pe", "SCL")
                        vsc = p.val("SCL")
                        p.wait("sync", "SCL", vsc)
                        for sub in range(4):
                            dst = _ap(sub_d, (ch_ * 512 + sub * 128) * D + mg * 512, [[D, 128], [1, 512]])
                            p.dma("sync", dst, bfs2[:, 2048 + 512 * sub: 2560 + 512 * sub], "STR")
                        p.gset("ffn_pe2", "STR")
                    p.gset("ffn_hnres", "TEN")

                p.wait("sync", "STR", p.val("STR"))
                for tt in range(8):
                    p.gwait("sync", "ffn_res")
                    p.dma("sync", stg2[:, 0:2048], xres_d[128 * tt:128 * (tt + 1), :], "LX")
                    ldv = p.dma("sync", bfs[:, 0:2048], sub_d[128 * tt:128 * (tt + 1), :], "LX")
                    p.wait("vector", "LX", ldv)
                    TT(stg2[:, 0:2048], stg2[:, 0:2048], bfs[:, 0:2048], ALU.add)
                    vv = p.val("VEC")
                    p.wait("sync", "VEC", vv)
                    if ph == 1:
                        p.dma("sync", out_x[128 * tt:128 * (tt + 1), :], stg2[:, 0:2048], "STR")
                        p.gset("ffn_res", "STR")
                    else:
                        p.dma("sync", xres_d[128 * tt:128 * (tt + 1), :], stg2[:, 0:2048], "SPL")
                        p.gset("ffn_res", "SPL")

            emit_ffn(0)
            ckpt(4)

            p.wait("sync", "SPL", p.val("SPL"))
            for tt, vln in ln_transpose_pass(xres_d, bf1, 1024, 8, 0):
                pass
            v_x3 = p.val("SCL")
            p.wait("tensor", "SCL", v_x3)
            p.wait("tensor", "VEC", p.val("VEC"))

            for pg4 in range(17):
                slot = pg4 % 2
                wsem = "LWA" if slot == 0 else "LWB"
                wslot = wbuf[:, 0:8192] if slot == 0 else bf1[:, 16384:24576]
                p0g = pg4 * 512
                pcg = 512 if pg4 < 16 else 320
                p.gwait("sync", ("b_w", slot))
                for j in range(8):
                    srcw = _ap(cc_qout, j * NQ + OFF_WI + p0g, [[PROJ_B, 128], [128 * PROJ_B, 2], [1, pcg]])
                    dst = _ap(wslot, j * 1024, [list(wslot.ap[0]), [512, 2], [1, pcg]])
                    p.dma("sync", dst, srcw, wsem)
                ldv = p.val(wsem)
                p.wait("tensor", wsem, ldv)
                for ph_ in range(2):
                  p0 = p0g + ph_ * 256
                  npt = 2 if p0 + 256 <= PROJ_B else 1
                  p.gwait("tensor", "b_ps")
                  for pt in range(npt):
                    mw = 128 if p0 + pt * 128 + 128 <= PROJ_B else 64
                    pss = [ps_a, ps_b] if pt == 0 else [ps_c, ps_d]
                    for ch_ in range(2):
                        for dbk in range(16):
                            MM(pss[ch_][0:mw, :],
                               wslot[:, dbk * 512 + ph_ * 256 + pt * 128: dbk * 512 + ph_ * 256 + pt * 128 + mw],
                               bf1[:, dbk * 1024 + ch_ * 512: dbk * 1024 + ch_ * 512 + 512], dbk == 0, dbk == 15,
                               sem=("TEN" if dbk == 15 else None))
                  if ph_ == 1:
                    p.gset(("b_w", slot), "TEN")
                  vt = p.val("TEN")
                  for pt in range(npt):
                    prow = p0 + pt * 128
                    pss = [ps_a, ps_b] if pt == 0 else [ps_c, ps_d]
                    p.wait("scalar", "TEN", vt)
                    if prow < I_B:
                        p.gwait("scalar", "b_sg")
                        ACT(bfs[:, 0:512], pss[0][:, :], AF.Silu)
                        ACT(bfs[:, 512:1024], pss[1][:, :], AF.Silu)
                        vsg = p.val("SCL")
                        p.wait("tensor", "SCL", vsg)
                        for sub in range(8):
                            tr_evac(bfs2[:, 128 * sub: 128 * (sub + 1)], bfs[:, 128 * sub:128 * (sub + 1)])
                            if sub == 7:
                                p.gset("b_sg", "SCL")
                        vsc = p.val("SCL")
                        p.wait("sync", "SCL", vsc)
                        p.gwait("sync", "b_sg2")
                        for sub in range(8):
                            t0 = (sub % 4) * 128 + (sub // 4) * 512
                            dst = _ap(sg_tp, t0 * I_B + prow, [[I_B, 128], [1, 128]])
                            p.dma("sync", dst, bfs2[:, 128 * sub: 128 * (sub + 1)], "STR")
                        p.gset("b_sg2", "STR")
                        p.gwait("scalar", "b_sg2")
                    elif prow < I_B + CONV_B:
                        xb_row = prow - I_B
                        cwt = xb_row // 128
                        p.gwait("scalar", "b_pre")
                        SCP(bfs[:, 3:515], pss[0][:, :])
                        SCP(bfs[:, 515:1027], pss[1][:, :])
                        vpre = p.val("SCL")
                        p.wait("vector", "SCL", vpre)
                        VCP(sb_sta[:, cwt * 6:cwt * 6 + 3], bfs[:, 3:6])
                        VCP(sb_sta[:, cwt * 6 + 3:cwt * 6 + 6], bfs[:, 1024:1027])
                        qcv = cwt % 2
                        cacc = stg2[:, qcv * 1024: qcv * 1024 + 1024]
                        p.gwait("vector", ("b_cv", qcv))
                        p.op("vector", lambda E, cwt=cwt, cacc=cacc: E.tensor_scalar(
                            cacc, bfs[:, 0:1024], sb_cwb[:, 4 * cwt:4 * cwt + 1], None, ALU.mult), "VEC")
                        for k2 in range(1, 4):
                            STT(cacc, bfs[:, k2:k2 + 1024], sb_cwb[:, 4 * cwt + k2:4 * cwt + k2 + 1],
                                cacc, ALU.mult, ALU.add)
                        vcv = p.val("VEC")
                        p.gset("b_pre", "VEC")
                        p.wait("scalar", "VEC", vcv)
                        if xb_row < I_B:
                            ACT(bf2[:, cwt * 1024: cwt * 1024 + 1024], cacc, AF.Silu, bias=sb_cbb[:, cwt:cwt + 1])
                        else:
                            which = (xb_row - I_B) // 128
                            ACT(bc_b[:, which * 1024:(which + 1) * 1024], cacc, AF.Silu, bias=sb_cbb[:, cwt:cwt + 1])
                        p.gset(("b_cv", qcv), "SCL")
                    else:
                        for chd in range(2):
                            ACT(stg[0:64, 1024 + chd * 512: 1536 + chd * 512], pss[chd][0:64, :], AF.Abs, bias=sb_dbb[:, :])
                            ACT(stg[0:64, 2048 + chd * 512: 2560 + chd * 512], pss[chd][0:64, :], AF.Relu, bias=sb_dbb[:, :])
                        vsc0 = p.val("SCL")
                        p.wait("scalar", "SCL", vsc0)
                        ACT(stg[0:64, 1024:2048], stg[0:64, 1024:2048], AF.Exp, scale=-1.0)
                        vsc1 = p.val("SCL")
                        p.wait("scalar", "SCL", vsc1)
                        ACT(stg[0:64, 1024:2048], stg[0:64, 1024:2048], AF.Ln, bias=1.0)
                        vsc2 = p.val("SCL")
                        p.wait("vector", "SCL", vsc2)
                        TT(sb_dt[:, 0:1024], stg[0:64, 1024:2048], stg[0:64, 2048:3072], ALU.add)
                  p.gset("b_ps", "SCL")
            v_inproj = p.val("SCL")

            p.wait("vector", "SCL", v_inproj)
            last3 = _ap(sb_sta[:, 0:204], 3, [list(sb_sta[:, 0:204].ap[0]), [6, 34], [1, 3]])
            VCP(bfs[:, 3072:3174], last3)
            vv = p.val("VEC")
            p.wait("sync", "VEC", vv)
            p.dma("sync", ag_hlI[:, :], bfs[:, 3072:3174], "SCC")
            p.wait("gpsimd", "SCC", p.val("SCC"))
            v_aghl = cc("AllGather", ag_hlI, ag_hlO)

            TS(sb_dt[:, :], sb_dt[:, :], 1e6, ALU.min)
            p.op("vector", lambda E: E.tensor_scalar(sb_sf[:, :], sb_dt[:, :], sb_anb[:, :], None, ALU.mult), "VEC")
            p.op("vector", lambda E: E.tensor_tensor_scan(sb_sf[:, :], sb_sf[:, :], sb_sf[:, :], 0.0, ALU.add, ALU.bypass), "VEC")
            TS(sb_ld[:, :], sb_dt[:, :], 1e-38, ALU.max)
            vv = p.val("VEC")
            p.wait("scalar", "VEC", vv)
            ACT(sb_ld[:, :], sb_ld[:, :], AF.Ln)
            v_dtfin = p.val("SCL")

            p.wait("sync", "CCS", v_aghl)
            for j in range(8):
                p.dma("sync", bfs[:, 3072 + 102 * j: 3174 + 102 * j], ag_hlO[128 * j:128 * (j + 1), :], "LX")
            ldv = p.val("LX")
            p.wait("vector", "LX", ldv)
            for j in range(8):
                if j == 0:
                    p.op("vector", lambda E: E.tensor_scalar(sb_hal[:, :], bfs[:, 3072:3174], sb_hse[:, 0:1], None, ALU.mult), "VEC")
                else:
                    STT(sb_hal[:, :], bfs[:, 3072 + 102 * j: 3174 + 102 * j], sb_hse[:, j:j + 1], sb_hal[:, :], ALU.mult, ALU.add)
                VF()
            p204 = list(stg2[:, 0:204].ap[0])
            fixb = lambda off, n3=3: _ap(stg2[:, 0:204], off, [p204, [6, 34], [1, n3]])
            VCP(fixb(0), sb_hal[:, 0:102])
            sta3 = _ap(sb_sta[:, 0:204], 0, [list(sb_sta[:, 0:204].ap[0]), [6, 34], [1, 3]])
            VCP(fixb(3), sta3)
            VF()
            facc = stg2[:, 256:358]
            facc3 = _ap(stg2[:, 256:358], 0, [p204, [3, 34], [1, 3]])
            tmp3 = _ap(stg2[:, 384:486], 0, [p204, [3, 34], [1, 3]])
            pw = list(sb_cwb[:, 0:136].ap[0])
            for k2 in range(4):
                wk = _ap(sb_cwb[:, 0:136], k2, [pw, [4, 34], [0, 3]])
                if k2 == 0:
                    TT(facc3, fixb(0), wk, ALU.mult)
                else:
                    TT(tmp3, fixb(k2), wk, ALU.mult)
                    VF()
                    TT(facc3, facc3, tmp3, ALU.add)
                    VF()
            cbb3 = _ap(sb_cbb[:, 0:34], 0, [list(sb_cbb[:, 0:34].ap[0]), [1, 34], [0, 3]])
            TT(facc3, facc3, cbb3, ALU.add)
            vcv = p.val("VEC")
            p.wait("scalar", "VEC", vcv)
            bf2p = bf2[:, 0:32768]
            out32 = _ap(bf2p, 0, [list(bf2p.ap[0]), [1024, 32], [1, 3]])
            in32 = _ap(stg2[:, 256:352], 0, [p204, [3, 32], [1, 3]])
            ACT(out32, in32, AF.Silu)
            outc = _ap(bc_b[:, 0:2048], 0, [list(bc_b[:, 0:2048].ap[0]), [1024, 2], [1, 3]])
            inc2 = _ap(stg2[:, 352:358], 0, [p204, [3, 2], [1, 3]])
            ACT(outc, inc2, AF.Silu)
            v_fix = p.val("SCL")

            p.wait("vector", "SCL", max(v_dtfin, v_fix))
            HRUN = big1[:, 0:4096]
            HIN = big1[:, 4096:8192]
            XTC = bf1[:, 16384:20480]
            DGA = bf1[:, 20480:28672]
            YCB = bf1[:, 28672:32768]
            HRB = wbuf[:, 0:4096]
            p.op("vector", lambda E: E.memset(HRUN, 0.0), "VEC")

            for c_ in range(NCH):
                c0 = c_ * CH
                p.gwait("vector", "A_srel")
                if c_ == 0:
                    VCP(stg2[0:64, 0:128], sb_sf[:, c0:c0 + CH])
                else:
                    p.op("vector", lambda E, c0=c0: E.tensor_scalar(
                        stg2[0:64, 0:128], sb_sf[:, c0:c0 + CH], sb_sf[:, c0 - 1:c0], None, ALU.subtract), "VEC")
                v_sr = p.val("VEC")
                p.wait("sync", "VEC", v_sr)
                p.dma("sync", sbc_d[:, :], stg2[0:64, 0:128], "STR")
                p.dma("sync", send_d[:, :], stg2[0:64, 127:128], "STR")
                p.wait("sync", "STR", p.val("STR"))
                p.gwait("sync", "A_sbc")
                r1 = p.dma("sync", stg[:, 0:4096], _ap(sbc_d, 0, [[0, 128], [1, 4096]]), "RPA")
                p.gwait("sync", "A_dto")
                r2 = p.dma("sync", sb_dto[:, :], _ap(send_d, 0, [[0, 128], [1, 64]]), "RPB")

                p.wait("tensor", "VEC", v_sr)
                tr_evac(sb_srt[:, :], stg2[0:64, 0:128], fp32=True)
                p.gset("A_srel", "SCL")
                p.op("scalar", lambda E: E.activation(sb_u[:, :], sb_srt[:, :], AF.Exp), "SCL")
                tr_evac(sb_ldT[:, :], sb_ld[:, c0:c0 + CH], fp32=True)
                tr_evac(bfs2[:, 8000:8128], bc_b[:, c0:c0 + CH])
                vsc = p.val("SCL")
                p.wait("vector", "SCL", vsc)
                TT(sb_bD[:, :], sb_ldT[:, :], sb_srt[:, :], ALU.subtract)
                p.wait("vector", "RPB", r2)
                TT(sb_wT[:, :], sb_dto[:, :], sb_srt[:, :], ALU.subtract)
                VF()
                TT(sb_wT[:, :], sb_wT[:, :], sb_ldT[:, :], ALU.add)
                v_bD = p.val("VEC")
                p.wait("scalar", "VEC", v_bD)
                ACT(sb_dto[:, :], sb_dto[:, :], AF.Exp)
                ACT(sb_wT[:, :], sb_wT[:, :], AF.Exp)
                v_w0 = p.val("SCL")
                p.wait("vector", "SCL", v_w0)
                v_dto = v_w0
                v_w = v_w0

                p.gwait("tensor", "A_xt")
                for xi in range(32):
                    tr_evac(XTC[:, xi * 128:(xi + 1) * 128], bf2[:, xi * 1024 + c0: xi * 1024 + c0 + CH])
                v_xt = p.val("SCL")
                p.gset("A_xt", "SCL")

                p.gwait("tensor", "A_gm")
                p.gwait("tensor", ("A_psb", 1))
                p.op("tensor", lambda E, c0=c0: E.matmul(ps_f[:, 0:128], bc_b[:, c0:c0 + CH],
                                                         bc_b[:, 1024 + c0: 1024 + c0 + CH], start=True, stop=True), "TEN", 1)
                vt = p.val("TEN")
                p.wait("vector", "TEN", vt)
                TT(stg2[:, 128:256], ps_f[:, 0:128], sb_tri[:, :], ALU.mult)
                v_gm = p.val("VEC")
                p.gset("A_gm", "VEC")

                p.wait("vector", "RPA", r1)
                p.gwait("vector", "A_dg0")
                p.wait("vector", "VEC", v_gm)
                gm3 = _ap(stg2[:, 128:256], 0, [list(stg2[:, 128:256].ap[0]), [0, 8], [1, 128]])
                for h0 in range(0, H_B, 8):
                    if h0 == 32:
                        p.wait("sync", "VEC", p.val("VEC"))
                        r1b = p.dma("sync", stg[:, 0:4096], _ap(sbc_d, 32 * 128, [[0, 128], [1, 4096]]), "RPA")
                        p.wait("vector", "RPA", r1b)
                    hloc = h0 % 32
                    mq = stg2[:, 2512:3536]
                    bD3 = _bc(sb_bD[:, h0:h0 + 8], 128)
                    srel8 = stg[:, hloc * 128: hloc * 128 + 1024]
                    TT(stg2[:, 512:1536], srel8, bD3, ALU.add)
                    VF()
                    TS(mq, stg2[:, 512:1536], 80.0, ALU.min)
                    vpe = p.val("VEC")
                    p.wait("scalar", "VEC", vpe)
                    ACT(mq, mq, AF.Exp)
                    vde = p.val("SCL")
                    p.wait("vector", "SCL", vde)
                    TT(DGA[:, h0 * 128:(h0 + 8) * 128], mq, gm3, ALU.mult)
                v_dg = p.val("VEC")
                p.gset("A_dg0", "VEC")
                p.gset("A_sbc", "VEC")

                p.wait("vector", "SCL", p.val("SCL"))
                VCP(HRB, HRUN)
                v_hrb = p.val("VEC")

                p.wait("tensor", "SCL", v_xt)
                p.wait("tensor", "VEC", max(v_dg, v_hrb))
                for blk in range(8):
                    par = blk % 2
                    psa = ps_a if par == 0 else ps_d
                    psb = ps_b if par == 0 else ps_f
                    p.gwait("tensor", ("A_psa", par))
                    for hj in range(8):
                        hh = blk * 8 + hj
                        MM(psa[:, hj * 64:(hj + 1) * 64], DGA[:, hh * 128:(hh + 1) * 128],
                           XTC[:, hh * 64:(hh + 1) * 64], True, True)
                    p.gwait("tensor", ("A_psb", par))
                    MM(psb[:, :], bc_b[:, 1024 + c0: 1024 + c0 + CH],
                       HRB[:, blk * 512:(blk + 1) * 512], True, True, sem="TEN")
                    vt = p.val("TEN")
                    p.wait("vector", "TEN", vt)
                    p.wait("vector", "SCL", v_w)
                    p.gwait("vector", "A_ep")
                    ub = _bc(sb_u[:, blk * 8: blk * 8 + 8], 64)
                    TT(stg2[:, 512:1024], psb[:, :], ub, ALU.mult)
                    p.gset(("A_psb", par), "VEC")
                    TT(stg2[:, 1024:1536], XTC[:, blk * 512:(blk + 1) * 512], sb_dxb[:, blk * 512:(blk + 1) * 512], ALU.mult)
                    TT(stg2[:, 512:1024], stg2[:, 512:1024], stg2[:, 1024:1536], ALU.add)
                    TT(YCB[:, blk * 512:(blk + 1) * 512], stg2[:, 512:1024], psa[:, :], ALU.add)
                    p.gset(("A_psa", par), "VEC")
                    p.gset("A_ep", "VEC")
                    p.gwait("vector", "A_bw8")
                    b3 = _ap(bfs2[:, 8000:8128], 0, [list(bfs2[:, 8000:8128].ap[0]), [0, 8], [1, 128]])
                    w3 = _bc(sb_wT[:, blk * 8: blk * 8 + 8], 128)
                    TT(bfs2[:, 4000:5024], b3, w3, ALU.mult)
                    vv = p.val("VEC")
                    p.wait("tensor", "VEC", vv)
                    for hj in range(8):
                        hh = blk * 8 + hj
                        MM(ps_c[:, hj * 64:(hj + 1) * 64], bfs2[:, 4000 + 128 * hj: 4128 + 128 * hj],
                           XTC[:, hh * 64:(hh + 1) * 64], True, True, sem=("TEN" if hj == 7 else None))
                    p.gset("A_bw8", "TEN")
                    vt2 = p.val("TEN")
                    p.wait("vector", "TEN", vt2)
                    dtb = _bc(sb_dto[:, blk * 8: blk * 8 + 8], 64)
                    TT(HRUN[:, blk * 512:(blk + 1) * 512], HRUN[:, blk * 512:(blk + 1) * 512], dtb, ALU.mult)
                    TT(HRUN[:, blk * 512:(blk + 1) * 512], HRUN[:, blk * 512:(blk + 1) * 512], ps_c[:, :], ALU.add)
                    p.gset("A_dto", "VEC")
                v_ych = p.val("VEC")
                p.wait("sync", "VEC", v_ych)
                if DBG and c_ == 0:
                    p.wait("sync", "SCL", p.val("SCL"))
                    p.dma("sync", _ap(dbg_t2, 0, [[16384, 128], [1, 64]]), sb_u[:, :], "STR")
                    p.dma("sync", _ap(dbg_t2, 64, [[16384, 128], [1, 64]]), sb_wT[:, :], "STR")
                    p.dma("sync", _ap(dbg_t2, 128, [[16384, 128], [1, 64]]), sb_dto[:, :], "STR")
                    p.dma("sync", _ap(dbg_t2, 192, [[16384, 128], [1, 64]]), sb_bD[:, :], "STR")
                    p.dma("sync", _ap(dbg_t2, 256, [[16384, 128], [1, 64]]), sb_ldT[:, :], "STR")
                    p.dma("sync", _ap(dbg_t2, 320, [[16384, 128], [1, 64]]), sb_srt[:, :], "STR")
                    p.dma("sync", _ap(dbg_t2, 384, [[16384, 128], [1, 128]]), stg2[:, 128:256], "STR")
                    p.dma("sync", _ap(dbg_t2, 512, [[16384, 128], [1, 4096]]), big1[:, 0:4096], "STR")
                    p.dma("sync", _ap(dbg_t1, 0, [[4096, 128], [1, 4096]]), DGA[:, 0:4096], "STR")
                p.dma("sync", y3_sp[c0:c0 + CH, :], YCB[:, 0:4096], "STR")
                p.gset("A_ycb", "STR")
                p.gwait("vector", "A_ycb")

            vv = p.val("VEC")
            p.wait("sync", "VEC", vv)
            p.dma("sync", ag_hI[:, :], HRUN, "SCC")
            VCP(stg2[0:64, 0:1], sb_sf[:, TPC - 1:TPC])
            vv = p.val("VEC")
            p.wait("sync", "VEC", vv)
            p.dma("sync", ag_sI[:, :], stg2[0:64, 0:1], "SCC")
            p.wait("gpsimd", "SCC", p.val("SCC"))
            cc("AllGather", ag_hI, ag_hO)
            v_agh = cc("AllGather", ag_sI, ag_sO)
            p.wait("sync", "CCS", v_agh)
            ldv = p.dma("sync", stg2[0:8, 0:64], _ap(ag_sO, 0, [[64, 8], [1, 64]]), "LX")
            p.wait("vector", "LX", ldv)
            VCP(bfs2[0:8, 200:264], stg2[0:8, 0:64])
            vv = p.val("VEC")
            p.wait("tensor", "VEC", vv)
            p.gwait("tensor", "trbank_f")
            p.op("tensor", lambda E: E.matmul(ps_e[0:8, 0:64], sb_sel[:, :], bfs2[0:8, 200:264],
                                              start=True, stop=True), "TEN", 1)
            vt = p.val("TEN")
            p.wait("scalar", "TEN", vt)
            ACT(stg2[0:8, 0:64], ps_e[0:8, 0:64], AF.Exp, bias=sb_seb[:, :])
            p.gset("trbank_f", "SCL")
            vsc = p.val("SCL")
            p.wait("sync", "SCL", vsc)
            p.dma("sync", w8_d[:, :], stg2[0:8, 0:64], "STR")
            p.wait("sync", "STR", p.val("STR"))
            r8 = p.dma("sync", sb_w8[:, :], _ap(w8_d, 0, [[0, 128], [1, 512]]), "RPB")
            p.op("vector", lambda E: E.memset(HIN, 0.0), "VEC")
            p.wait("vector", "RPB", r8)
            for j in range(8):
                hsem = "LWA" if j % 2 == 0 else "LWB"
                p.gwait("sync", ("hc", j % 2))
                buf = stg[:, 0:4096] if j % 2 == 0 else stg2[:, 0:4096]
                ldv = p.dma("sync", buf, ag_hO[128 * j:128 * (j + 1), :], hsem)
                p.wait("vector", hsem, ldv)
                wj = _bc(sb_w8[:, j * 64:(j + 1) * 64], 64)
                TT(buf, buf, wj, ALU.mult)
                TT(HIN, HIN, buf, ALU.add)
                p.gset(("hc", j % 2), "VEC")
            HINB = wbuf[:, 4096:8192]
            VCP(HINB, HIN)
            v_hin = p.val("VEC")
            p.wait("sync", "VEC", v_hin)
            rnw = p.dma("gpsimd", sb_dxb[:, :], _ap(nrmw, 0, [[0, 128], [1, I_B]]), "LD2")
            p.wait("vector", "LD2", rnw)

            for c_ in range(NCH):
                c0 = c_ * CH
                p.gwait("sync", "B_y")
                ldy = p.dma("sync", bf1[:, 16384:20480], y3_sp[c0:c0 + CH, :], "LH")
                lds = p.dma("sync", bf1[:, 20480:24576], sg_tp[c0:c0 + CH, :], "LH")
                tr_evac(sb_uf[:, 0:64], sb_sf[:, c0:c0 + CH], fp32=True)
                p.op("scalar", lambda E: E.activation(sb_uf[:, :], sb_uf[:, :], AF.Exp), "SCL")
                v_uf = p.val("SCL")
                p.wait("tensor", "VEC", v_hin)
                for blk in range(8):
                    par = blk % 2
                    psa = ps_a if par == 0 else ps_b
                    p.gwait("tensor", ("B_psa", par))
                    MM(psa[:, :], bc_b[:, 1024 + c0: 1024 + c0 + CH],
                       HINB[:, blk * 512:(blk + 1) * 512], True, True, sem="TEN")
                    vt = p.val("TEN")
                    p.wait("vector", "TEN", vt)
                    p.wait("vector", "SCL", v_uf)
                    p.wait("vector", "LH", lds)
                    ufb = _bc(sb_uf[:, blk * 8: blk * 8 + 8], 64)
                    TT(stg2[:, 0:512], psa[:, :], ufb, ALU.mult)
                    p.gset(("B_psa", par), "VEC")
                    TT(stg2[:, 512:1024], stg2[:, 0:512], bf1[:, 16384 + blk * 512: 16896 + blk * 512], ALU.add)
                    TT(big1[:, blk * 512:(blk + 1) * 512], stg2[:, 512:1024], bf1[:, 20480 + blk * 512: 20992 + blk * 512], ALU.mult)
                v_yg = p.val("VEC")
                p.gset("B_y", "VEC")
                p.wait("scalar", "VEC", v_yg)
                for blk in range(8):
                    ACT(stg2[:, 1024:1536], big1[:, blk * 512:(blk + 1) * 512], AF.Square,
                        accum_out=stg2[:, 3500 + blk:3501 + blk])
                vsq = p.val("SCL")
                p.wait("vector", "SCL", vsq)
                p.op("vector", lambda E: E.tensor_reduce(stg2[:, 3520:3521], stg2[:, 3500:3508], AX, ALU.add), "VEC")
                VF()
                TS(stg2[:, 3520:3521], stg2[:, 3520:3521], 1.0 / I_B, ALU.mult)
                vv = p.val("VEC")
                p.wait("scalar", "VEC", vv)
                ACT(stg2[:, 3520:3521], stg2[:, 3520:3521], AF.Sqrt, bias=sb_eps[:, 2:3])
                vsc = p.val("SCL")
                p.wait("vector", "SCL", vsc)
                p.op("vector", lambda E: E.reciprocal(stg2[:, 3520:3521], stg2[:, 3520:3521]), "VEC")
                VF()
                for blk in range(8):
                    STT(bfs[:, 4096 + blk * 512: 4608 + blk * 512], big1[:, blk * 512:(blk + 1) * 512],
                        stg2[:, 3520:3521], sb_dxb[:, blk * 512:(blk + 1) * 512], ALU.mult, ALU.mult)
                v_yn = p.val("VEC")
                p.wait("tensor", "VEC", v_yn)
                p.gwait("tensor", "B_ynT")
                for sub in range(32):
                    tr_evac(bf2[:, sub * 1024 + c0: sub * 1024 + c0 + CH], bfs[:, 4096 + 128 * sub: 4224 + 128 * sub])
                p.gset("B_ynT", "SCL")
                p.gwait("vector", "B_ynT")
            v_ynT = p.val("SCL")

            p.wait("tensor", "SCL", v_ynT)
            PBK2 = (ps_a, ps_b, ps_c, ps_d)
            for mg in range(4):
                for ch_ in range(2):
                    for k in range(32):
                        sl = (k // 4) % 2
                        wsem = "LWA" if sl == 0 else "LWB"
                        if k % 4 == 0:
                            p.gwait("sync", ("o_w", sl))
                            srcw = _ap(cc_qout, (k // 4) * NQ + OFF_WO + mg * 512, [[D, 128], [128 * D, 4], [1, 512]])
                            ldv = p.dma("sync", wbuf[:, sl * 2048: sl * 2048 + 2048], srcw, wsem)
                            p.wait("tensor", wsem, ldv)
                        wap = wbuf[:, sl * 2048 + (k % 4) * 512: sl * 2048 + (k % 4) * 512 + 512]
                        if k == 0:
                            for mi in range(4):
                                p.gwait("tensor", ("o_ps", mi))
                        for mi in range(4):
                            MM(PBK2[mi][:, :], wap[:, mi * 128: mi * 128 + 128],
                               bf2[:, k * 1024 + ch_ * 512: k * 1024 + ch_ * 512 + 512], k == 0, k == 31,
                               sem=("TEN" if mi == 3 else None))
                        if k % 4 == 3:
                            p.gset(("o_w", sl), "TEN")
                    vt = p.val("TEN")
                    p.wait("scalar", "TEN", vt)
                    for mi in range(4):
                        m = mg * 4 + mi
                        p.gwait("scalar", ("o_e", mi % 2))
                        SCP(bfs[:, (mi % 2) * 1024: (mi % 2) * 1024 + 512], PBK2[mi][:, :])
                        vsc = p.val("SCL")
                        p.gset(("o_ps", mi), "SCL")
                        p.wait("tensor", "SCL", vsc)
                        for sub in range(4):
                            tr_evac(bfs2[:, 2048 + 128 * sub: 2176 + 128 * sub],
                                    bfs[:, (mi % 2) * 1024 + 128 * sub: (mi % 2) * 1024 + 128 * (sub + 1)])
                        p.gset(("o_e", mi % 2), "SCL")
                        vsc = p.val("SCL")
                        p.wait("sync", "SCL", vsc)
                        p.gwait("sync", "o_st")
                        for sub in range(4):
                            dst = _ap(sub_d, (ch_ * 512 + sub * 128) * D + m * 128, [[D, 128], [1, 128]])
                            p.dma("sync", dst, bfs2[:, 2048 + 128 * sub: 2176 + 128 * sub], "STR")
                        p.gset("o_st", "STR")
                        p.gwait("scalar", "o_st")

            p.wait("sync", "STR", p.val("STR"))
            for tt in range(8):
                p.gwait("sync", "b_res")
                p.dma("sync", stg2[:, 0:2048], xres_d[128 * tt:128 * (tt + 1), :], "LX")
                ldv = p.dma("sync", bfs[:, 0:2048], sub_d[128 * tt:128 * (tt + 1), :], "LX")
                p.wait("vector", "LX", ldv)
                TT(stg2[:, 0:2048], stg2[:, 0:2048], bfs[:, 0:2048], ALU.add)
                vv = p.val("VEC")
                p.wait("sync", "VEC", vv)
                p.dma("sync", xres_d[128 * tt:128 * (tt + 1), :], stg2[:, 0:2048], "SPL")
                p.gset("b_res", "SPL")

            ckpt(5)

            emit_ffn(1)
            wrote_out[0] = True

        except _StopEmission:
            pass

        if not wrote_out[0]:
            p.wait("sync", "SPL", p.val("SPL"))
            if PHASES < 2:
                p.op("vector", lambda E: E.memset(stg2[:, 0:2048], 0.0), "VEC")
                p.wait("sync", "VEC", p.val("VEC"))
                for tt in range(8):
                    p.dma("sync", out_x[128 * tt:128 * (tt + 1), :], stg2[:, 0:2048], "STR")
            else:
                for tt in range(8):
                    ldv = p.dma("sync", stg2[:, 0:2048], xres_d[128 * tt:128 * (tt + 1), :], "LX")
                    p.wait("sync", "LX", ldv)
                    p.dma("sync", out_x[128 * tt:128 * (tt + 1), :], stg2[:, 0:2048], "STR")

        if DBG:
            p.wait("sync", "CCS", p.val("CCS"))
            p.dma("sync", dbg_qout.ap(), cc_qout.ap(), "STR")
            if PHASES >= 2:
                p.dma("sync", dbg_dO.ap(), a2a_dO.ap(), "STR")
                p.dma("sync", dbg_xO.ap(), a2a_xO.ap(), "STR")
                p.dma("sync", dbg_bcO.ap(), ag_bcO.ap(), "STR")
                p.dma("sync", dbg_yO.ap(), a2a_yO.ap(), "STR")
        p.wait("sync", "STR", p.val("STR"))
        p.emit(block)

    return nc


def stage_inputs(inputs):
    x = np.asarray(inputs["L0_attn_a_x"], np.float32).reshape(B * T, D)
    gu = np.asarray(inputs["L0_ffn_w_gate_up"], np.float32)
    pj = np.asarray(inputs["L0_ffn_w_proj"], np.float32)
    win = np.asarray(inputs["L0_attn_b_W_in"], np.float32)
    wot = np.asarray(inputs["L0_attn_b_W_out"], np.float32)
    winT = np.ascontiguousarray(win.T)
    wotT = np.ascontiguousarray(wot.T)
    xpjT = np.ascontiguousarray(np.asarray(inputs["L0_attn_a_xproj_w"], np.float32).T)
    dtpT = np.ascontiguousarray(np.asarray(inputs["L0_attn_a_dtproj_w"], np.float32).T)
    cwa = np.ascontiguousarray(np.asarray(inputs["L0_attn_a_conv_w"], np.float32)[:, 0, :])
    cba = np.asarray(inputs["L0_attn_a_conv_b"], np.float32).reshape(D, 1)
    dtba = np.asarray(inputs["L0_attn_a_dtproj_b"], np.float32).reshape(D, 1)
    alga = np.asarray(inputs["L0_attn_a_A_log"], np.float32)
    dpar = np.asarray(inputs["L0_attn_a_D_param"], np.float32).reshape(1, D)
    cwb = np.asarray(inputs["L0_attn_b_conv_weight"], np.float32)
    cbb = np.asarray(inputs["L0_attn_b_conv_bias"], np.float32).reshape(CONV_B, 1)
    dtbb = np.asarray(inputs["L0_attn_b_dt_bias"], np.float32).reshape(H_B, 1)
    algb = np.asarray(inputs["L0_attn_b_A_log"], np.float32).reshape(H_B, 1)
    dexp = np.repeat(np.asarray(inputs["L0_attn_b_D"], np.float32), 64).reshape(1, I_B)
    nrmw = np.asarray(inputs["L0_attn_b_norm_weight"], np.float32).reshape(1, I_B)
    idb = np.eye(128, dtype=ml_dtypes.bfloat16)
    idf = np.eye(128, dtype=np.float32)
    tri = np.ascontiguousarray(np.triu(np.ones((128, 128), np.float32)).astype(ml_dtypes.bfloat16))
    maps = []
    for c in range(NCORES):
        t0 = c * TPC
        halo = np.zeros((128, D), np.float32) if c % 4 == 0 else x[t0 - 128:t0]
        xhc = np.concatenate([halo, x[t0:t0 + TPC]], axis=0)
        sel = np.zeros((8, 8), ml_dtypes.bfloat16)
        bias = np.full((8, 1), -1e30, np.float32)
        g0 = (c // 4) * 4
        for j in range(g0, c):
            bias[j, 0] = 0.0
            for k in range(j + 1, c):
                sel[k, j] = 1.0
        hs = np.zeros((1, 8), np.float32)
        if c % 4 != 0:
            hs[0, c - 1] = 1.0
        maps.append(dict(
            xh=np.ascontiguousarray(xhc),
            guw=np.ascontiguousarray(gu[:, c * DSL:(c + 1) * DSL]),
            pjw=np.ascontiguousarray(pj[:, c * 1024:(c + 1) * 1024]),
            wint=np.ascontiguousarray(winT[c * DSL:(c + 1) * DSL, :]),
            wott=np.ascontiguousarray(wotT[c * 512:(c + 1) * 512, :]),
            xpjt=xpjT, dtpt=dtpT, cwa=cwa, cba=cba, dtba=dtba,
            alga=np.ascontiguousarray(alga[c * DSL:(c + 1) * DSL, :]),
            dpar=dpar, cwb=cwb, cbb=cbb, dtbb=dtbb, algb=algb,
            dexp=dexp, nrmw=nrmw, identb=idb, identf=idf, tri01=tri,
            selm=sel, selb=bias, hsel=hs,
            epsc=np.array([[EPS_LN, EPS_FFN, EPS_B, 1.0 / HF, 0, 0, 0, 0]], np.float32),
        ))
    return maps


def kernel(**inputs):
    maps = stage_inputs(inputs)
    with allow_ici_all_to_all():
        nc = build()
        res = run_bass_kernel_spmd(nc, maps, core_ids=list(range(NCORES)))
    out = np.concatenate([np.asarray(res.results[c]["out"], np.float32) for c in range(NCORES)], axis=0)
    return out.reshape(B, T, D)


# version 14 (final)
# speedup vs baseline: 1.0278x; 1.0278x over previous
import numpy as np
import ml_dtypes
from contextlib import ExitStack

import concourse.bass as bass
import concourse.mybir as mybir
from concourse.bass_utils import run_bass_kernel_spmd
from concourse.aot_env import allow_ici_all_to_all

F32 = mybir.dt.float32
BF16 = mybir.dt.bfloat16
AF = mybir.ActivationFunctionType
ALU = mybir.AluOpType
AX = mybir.AxisListType.X

NCORES = 8
B, T, D = 2, 4096, 2048
TPC = 1024
DSL = 256
NST = 16
I_B = 4096
H_B = 64
N_B = 128
CONV_B = 4352
PROJ_B = 8512
CH = 128
NCH = TPC // CH
HF2 = 16384
HF = 8192
EPS_LN = 1e-5
EPS_FFN = float(np.finfo(np.float32).eps)
EPS_B = 1e-6
SCLAMP = -60.0

OFF_GU = 0
OFF_PJ = OFF_GU + DSL * HF2
OFF_WI = OFF_PJ + 1024 * D
OFF_WO = OFF_WI + DSL * PROJ_B
NQ = OFF_WO + 512 * D

DBG = False
PHASES = 5
PREP_LVL = 3


class _StopEmission(Exception):
    pass


def _ap(t, off, dims):
    a = t if isinstance(t, bass.AP) else t.ap()
    return bass.AP(a.tensor, a.offset + off, [list(d) for d in dims])


def _bc(ap2d, nrep):
    return bass.AP(ap2d.tensor, ap2d.offset, [list(ap2d.ap[0]), list(ap2d.ap[1]), [0, nrep]])


class Prog:
    def __init__(self, nc, ctx):
        self.nc, self.ctx = nc, ctx
        self.qs = {k: [] for k in ("sync", "gpsimd", "vector", "scalar", "tensor")}
        self.sems = {}
        self.guards = {}

    def sem(self, name):
        self.sems[name] = [self.ctx.enter_context(self.nc.semaphore(name)), 0]

    def val(self, s):
        return self.sems[s][1]

    def op(self, q, fn, sem=None, k=1):
        if sem is None:
            self.qs[q].append(fn)
            return None
        rec = self.sems[sem]
        rec[1] += k
        h = rec[0]
        self.qs[q].append(lambda E, fn=fn, h=h, k=k: fn(E).then_inc(h, k))
        return rec[1]

    def dma(self, q, out, in_, sem=None):
        return self.op(q, lambda E: E.dma_start(out=out, in_=in_), sem, 16)

    def wait(self, q, sem, v=None):
        rec = self.sems[sem]
        if v is None:
            v = rec[1]
        if v <= 0:
            return
        h = rec[0]
        self.qs[q].append(lambda E, h=h, v=v: E.wait_ge(h, v))

    def gset(self, key, sem):
        self.guards[key] = (sem, self.sems[sem][1])

    def gwait(self, q, key):
        if key in self.guards:
            s, v = self.guards[key]
            self.wait(q, s, v)

    def emit(self, block):
        qs = self.qs

        @block.sync
        def _(E):
            for f in qs["sync"]:
                f(E)

        @block.gpsimd
        def _(E):
            for f in qs["gpsimd"]:
                f(E)

        @block.vector
        def _(E):
            for f in qs["vector"]:
                f(E)

        @block.scalar
        def _(E):
            for f in qs["scalar"]:
                f(E)

        @block.tensor
        def _(E):
            for f in qs["tensor"]:
                f(E)


def build():
    nc = bass.Bass("TRN2", target_bir_lowering=False, detect_race_conditions=False)

    EI = dict(kind="ExternalInput")
    xh = nc.dram_tensor("xh", [TPC + 128, D], F32, **EI)
    guw = nc.dram_tensor("guw", [HF2, DSL], F32, **EI)
    pjw = nc.dram_tensor("pjw", [D, 1024], F32, **EI)
    wint = nc.dram_tensor("wint", [DSL, PROJ_B], F32, **EI)
    wott = nc.dram_tensor("wott", [512, D], F32, **EI)
    xpjt = nc.dram_tensor("xpjt", [D, 160], F32, **EI)
    dtpt = nc.dram_tensor("dtpt", [128, D], F32, **EI)
    cwa = nc.dram_tensor("cwa", [D, 4], F32, **EI)
    cba = nc.dram_tensor("cba", [D, 1], F32, **EI)
    dtba = nc.dram_tensor("dtba", [D, 1], F32, **EI)
    alga = nc.dram_tensor("alga", [DSL, NST], F32, **EI)
    dpar = nc.dram_tensor("dpar", [1, D], F32, **EI)
    cwb = nc.dram_tensor("cwb", [CONV_B, 4], F32, **EI)
    cbb = nc.dram_tensor("cbb", [CONV_B, 1], F32, **EI)
    dtbb = nc.dram_tensor("dtbb", [H_B, 1], F32, **EI)
    algb = nc.dram_tensor("algb", [H_B, 1], F32, **EI)
    dexp = nc.dram_tensor("dexp", [1, I_B], F32, **EI)
    nrmw = nc.dram_tensor("nrmw", [1, I_B], F32, **EI)
    identb = nc.dram_tensor("identb", [128, 128], BF16, **EI)
    identf = nc.dram_tensor("identf", [128, 128], F32, **EI)
    tri01 = nc.dram_tensor("tri01", [128, 128], BF16, **EI)
    selm = nc.dram_tensor("selm", [8, 8], BF16, **EI)
    selb = nc.dram_tensor("selb", [8, 1], F32, **EI)
    hsel = nc.dram_tensor("hsel", [1, 8], F32, **EI)
    epsc = nc.dram_tensor("epsc", [1, 8], F32, **EI)

    out_x = nc.dram_tensor("out", [TPC, D], F32, kind="ExternalOutput")

    KI = dict(kind="ExternalOutput" if DBG else "Internal")
    xres_d = nc.dram_tensor("xres_d", [TPC, D], F32, **KI)
    cc_qin = nc.dram_tensor("cc_qin", [NQ], BF16, kind="Internal")
    cc_qout = nc.dram_tensor("cc_qout", [8 * NQ], BF16, kind="Internal", addr_space="Shared")
    a2a_dI = nc.dram_tensor("a2a_dI", [D, TPC], F32, kind="Internal")
    a2a_dO = nc.dram_tensor("a2a_dO", [D, TPC], F32, kind="Internal")
    a2a_xI = nc.dram_tensor("a2a_xI", [D, TPC], BF16, kind="Internal")
    a2a_xO = nc.dram_tensor("a2a_xO", [D, TPC], BF16, kind="Internal")
    ag_bcI = nc.dram_tensor("ag_bcI", [32, TPC], BF16, kind="Internal")
    ag_bcO = nc.dram_tensor("ag_bcO", [8 * 32, TPC], BF16, kind="Internal", addr_space="Shared")
    a2a_yI = nc.dram_tensor("a2a_yI", [D, TPC], BF16, kind="Internal")
    a2a_yO = nc.dram_tensor("a2a_yO", [D, TPC], BF16, kind="Internal")
    ag_hlI = nc.dram_tensor("ag_hlI", [128, 102], BF16, kind="Internal")
    ag_hlO = nc.dram_tensor("ag_hlO", [8 * 128, 102], BF16, kind="Internal", addr_space="Shared")
    ag_hI = nc.dram_tensor("ag_hI", [N_B, H_B * 64], F32, kind="Internal")
    ag_hO = nc.dram_tensor("ag_hO", [8 * N_B, H_B * 64], F32, kind="Internal", addr_space="Shared")
    ag_sI = nc.dram_tensor("ag_sI", [H_B, 1], F32, kind="Internal")
    ag_sO = nc.dram_tensor("ag_sO", [8 * H_B, 1], F32, kind="Internal", addr_space="Shared")
    sg_tp = nc.dram_tensor("sg_tp", [TPC, I_B], BF16, kind="Internal")
    y3_sp = nc.dram_tensor("y3_sp", [TPC, I_B], BF16, **KI)
    hn_d = nc.dram_tensor("hn_d", [HF, TPC], BF16, kind="Internal")
    sub_d = nc.dram_tensor("sub_d", [TPC, D], BF16, **KI)
    inv_d = nc.dram_tensor("inv_d", [1, TPC], F32, kind="Internal")
    sbc_d = nc.dram_tensor("sbc_d", [H_B, CH], F32, kind="Internal")
    if DBG:
        dbg_qout = nc.dram_tensor("dbg_qout", [8 * NQ], BF16, kind="ExternalOutput")
        dbg_dO = nc.dram_tensor("dbg_dO", [D, TPC], F32, kind="ExternalOutput")
        dbg_xO = nc.dram_tensor("dbg_xO", [D, TPC], BF16, kind="ExternalOutput")
        dbg_bcO = nc.dram_tensor("dbg_bcO", [8 * 32, TPC], BF16, kind="ExternalOutput")
        dbg_yO = nc.dram_tensor("dbg_yO", [D, TPC], BF16, kind="ExternalOutput")
        dbg_t1 = nc.dram_tensor("dbg_t1", [128, 4096], BF16, kind="ExternalOutput")
        dbg_t2 = nc.dram_tensor("dbg_t2", [128, 16384], F32, kind="ExternalOutput")
    send_d = nc.dram_tensor("send_d", [H_B, 1], F32, kind="Internal")
    w8_d = nc.dram_tensor("w8_d", [8, H_B], F32, kind="Internal")

    with ExitStack() as ctx:
        En = ctx.enter_context

        sb_id = En(nc.sbuf_tensor("sb_id", [128, 128], BF16))
        sb_idf = En(nc.sbuf_tensor("sb_idf", [128, 128], F32))
        sb_tri = En(nc.sbuf_tensor("sb_tri", [128, 128], BF16))
        sb_cwa = En(nc.sbuf_tensor("sb_cwa", [128, 64], F32))
        sb_cba = En(nc.sbuf_tensor("sb_cba", [128, 16], F32))
        sb_dtb = En(nc.sbuf_tensor("sb_dtb", [128, 16], F32))
        sb_alg = En(nc.sbuf_tensor("sb_alg", [128, 2 * NST], F32))
        sb_cwb = En(nc.sbuf_tensor("sb_cwb", [128, 136], F32))
        sb_cbb = En(nc.sbuf_tensor("sb_cbb", [128, 34], F32))
        sb_dbb = En(nc.sbuf_tensor("sb_dbb", [H_B, 1], F32))
        sb_anb = En(nc.sbuf_tensor("sb_anb", [H_B, 1], F32))
        sb_dxb = En(nc.sbuf_tensor("sb_dxb", [128, I_B], BF16))
        sb_sel = En(nc.sbuf_tensor("sb_sel", [8, 8], BF16))
        sb_seb = En(nc.sbuf_tensor("sb_seb", [8, 1], F32))
        sb_hse = En(nc.sbuf_tensor("sb_hse", [128, 8], F32))
        sb_eps = En(nc.sbuf_tensor("sb_eps", [128, 8], F32))
        sb_one = En(nc.sbuf_tensor("sb_one", [128, 1], BF16))

        wbuf = En(nc.sbuf_tensor("wbuf", [128, 8192], BF16))
        stg = En(nc.sbuf_tensor("stg", [128, 4096], F32))
        stg2 = En(nc.sbuf_tensor("stg2", [128, 4096], F32))
        big1 = En(nc.sbuf_tensor("big1", [128, 16384], F32))
        big2 = En(nc.sbuf_tensor("big2", [128, 16384], F32))
        bc_b = En(nc.sbuf_tensor("bc_b", [128, 2 * TPC], BF16))
        sb_dt = En(nc.sbuf_tensor("sb_dt", [H_B, TPC], F32))
        sb_sf = En(nc.sbuf_tensor("sb_sf", [H_B, TPC], F32))
        sb_ld = En(nc.sbuf_tensor("sb_ld", [H_B, TPC], F32))
        sb_srt = En(nc.sbuf_tensor("sb_srt", [128, H_B], F32))
        sb_u = En(nc.sbuf_tensor("sb_u", [128, H_B], F32))
        sb_uf = En(nc.sbuf_tensor("sb_uf", [128, H_B], F32))
        sb_ldT = En(nc.sbuf_tensor("sb_ldT", [128, H_B], F32))
        sb_wT = En(nc.sbuf_tensor("sb_wT", [128, H_B], F32))
        sb_bD = En(nc.sbuf_tensor("sb_bD", [128, H_B], F32))
        sb_dto = En(nc.sbuf_tensor("sb_dto", [128, H_B], F32))
        sb_w8 = En(nc.sbuf_tensor("sb_w8", [128, 512], F32))
        sb_hal = En(nc.sbuf_tensor("sb_hal", [128, 102], F32))
        sb_sta = En(nc.sbuf_tensor("sb_sta", [128, 204], BF16))

        ps_a = En(nc.psum_tensor("ps_a", [128, 512], F32))
        ps_b = En(nc.psum_tensor("ps_b", [128, 512], F32))
        ps_c = En(nc.psum_tensor("ps_c", [128, 512], F32))
        ps_d = En(nc.psum_tensor("ps_d", [128, 512], F32))
        ps_e = En(nc.psum_tensor("ps_e", [128, 512], F32))
        ps_f = En(nc.psum_tensor("ps_f", [128, 512], F32))
        ps_t = En(nc.psum_tensor("ps_t", [128, 512], F32))
        ps_u = En(nc.psum_tensor("ps_u", [128, 512], F32))

        p = Prog(nc, ctx)
        for s in ("LD", "LD2", "VEC", "SCL", "TEN", "CCS", "STR", "SPL", "RPL",
                  "LQ", "LS", "LX", "LH", "LWA", "LWB", "RPA", "RPB", "SCQ", "SCC"):
            p.sem(s)
        block = En(nc.Block())

        bf1 = big1[:, :].bitcast(BF16)
        bf2 = big2[:, :].bitcast(BF16)
        bfs = stg[:, :].bitcast(BF16)
        bfs2 = stg2[:, :].bitcast(BF16)

        RG = [list(range(NCORES))]

        def cc(kind, tin, tout):
            return p.op("gpsimd",
                        lambda E: E.collective_compute(kind, ALU.bypass, replica_groups=RG,
                                                       ins=[tin.ap().opt()], outs=[tout.ap().opt()]),
                        "CCS", 1)

        def TT(out, a, b, alu):
            return p.op("vector", lambda E: E.tensor_tensor(out, a, b, alu), "VEC")

        def TS(out, a, s1, alu):
            return p.op("vector", lambda E: E.tensor_single_scalar(out, a, s1, alu), "VEC")

        def TS2(out, a, s1, s2, alu0, alu1):
            return p.op("vector", lambda E: E.tensor_scalar(out, a, s1, s2, alu0, alu1), "VEC")

        def STT(out, a, s1, b, alu0, alu1):
            return p.op("vector", lambda E: E.scalar_tensor_tensor(out, a, s1, b, alu0, alu1), "VEC")

        def VCP(out, in_):
            return p.op("vector", lambda E: E.tensor_copy(out, in_), "VEC")

        def ACT(out, in_, f, bias=0.0, scale=1.0, accum_out=None):
            if accum_out is not None:
                return p.op("scalar", lambda E: E.activation(out, in_, f, bias=bias, scale=scale, accum_out=accum_out), "SCL")
            return p.op("scalar", lambda E: E.activation(out, in_, f, bias=bias, scale=scale), "SCL")

        def SCP(out, in_):
            return p.op("scalar", lambda E: E.copy(out, in_), "SCL")

        def MM(out, lhsT, rhs, st_, sp_, sem=None):
            return p.op("tensor", lambda E: E.matmul(out, lhsT, rhs, start=st_, stop=sp_), sem, 1)

        def VF():
            p.wait("vector", "VEC", p.val("VEC"))

        tr_state = {"b": 0}

        def tr_group(items, fp32=False, evac="scalar"):
            idt = sb_idf if fp32 else sb_id
            for g0 in range(0, len(items), 4):
                grp = items[g0:g0 + 4]
                if fp32:
                    bank = ps_e[:, :]
                    key = "trbank_f"
                else:
                    bank = (ps_t if tr_state["b"] == 0 else ps_u)[:, :].bitcast(BF16)
                    key = ("trbank", tr_state["b"])
                    tr_state["b"] ^= 1
                p.gwait("tensor", key)
                outs = []
                for sl, (dst, srcap) in enumerate(grp):
                    np_ = srcap.ap[0][1]
                    wid = srcap.ap[1][1]
                    outp = bass.AP(bank.tensor, bank.offset + 128 * sl, [[bank.ap[0][0], wid], [1, np_]])
                    p.op("tensor", lambda E, outp=outp, srcap=srcap, idt=idt, np_=np_: E.transpose(
                        outp, srcap, idt[0:np_, 0:np_]), "TEN", 1)
                    outs.append((dst, outp))
                vt = p.val("TEN")
                p.wait(evac, "TEN", vt)
                for dst, outp in outs:
                    if evac == "scalar":
                        SCP(dst, outp)
                    else:
                        VCP(dst, outp)
                p.gset(key, "SCL" if evac == "scalar" else "VEC")

        def tr_evac(dst, src, fp32=False, evac="scalar"):
            tr_group([(dst, src)], fp32=fp32, evac=evac)

        p.dma("sync", sb_id[:, :], identb.ap(), "LD")
        p.dma("sync", sb_idf[:, :], identf.ap(), "LD")
        p.dma("sync", sb_tri[:, :], tri01.ap(), "LD")
        for k in range(16):
            p.dma("sync", sb_cwa[:, 4 * k:4 * k + 4], cwa[128 * k:128 * (k + 1), :], "LD")
            p.dma("sync", sb_cba[:, k:k + 1], cba[128 * k:128 * (k + 1), :], "LD")
            p.dma("sync", sb_dtb[:, k:k + 1], dtba[128 * k:128 * (k + 1), :], "LD")
        p.dma("sync", sb_alg[:, 0:NST], alga[0:128, :], "LD")
        p.dma("sync", sb_alg[:, NST:2 * NST], alga[128:256, :], "LD")
        for k in range(34):
            p.dma("sync", sb_cwb[:, 4 * k:4 * k + 4], cwb[128 * k:128 * (k + 1), :], "LD")
            p.dma("sync", sb_cbb[:, k:k + 1], cbb[128 * k:128 * (k + 1), :], "LD")
        p.dma("sync", sb_dbb[:, :], dtbb.ap(), "LD")
        p.dma("sync", sb_anb[:, :], algb.ap(), "LD")
        p.dma("gpsimd", sb_dxb[:, :], _ap(dexp, 0, [[0, 128], [1, I_B]]), "LD2")
        p.dma("sync", sb_sel[:, :], selm.ap(), "LD")
        p.dma("sync", sb_seb[:, :], selb.ap(), "LD")
        p.dma("sync", sb_hse[:, :], _ap(hsel, 0, [[0, 128], [1, 8]]), "LD")
        p.dma("sync", sb_eps[:, :], _ap(epsc, 0, [[0, 128], [1, 8]]), "LD")
        for k in range(16):
            p.dma("gpsimd", wbuf[:, 160 * k:160 * (k + 1)], xpjt[128 * k:128 * (k + 1), :], "LD2")
        p.dma("gpsimd", wbuf[:, 2560:4608], dtpt[:, :], "LD2")
        v_small = p.val("LD")
        v_small2 = p.val("LD2")

        p.wait("scalar", "LD", v_small)
        ACT(stg[:, 0:2 * NST], sb_alg[:, :], AF.Exp)
        ACT(stg[0:H_B, 40:41], sb_anb[:, :], AF.Exp)
        v0 = p.val("SCL")
        p.wait("vector", "SCL", v0)
        TS(sb_alg[:, :], stg[:, 0:2 * NST], -1.0, ALU.mult)
        TS(sb_anb[:, :], stg[0:H_B, 40:41], -1.0, ALU.mult)
        p.op("vector", lambda E: E.memset(sb_one[:, :], 1.0), "VEC")

        def quant_ops(width, srccol=0, freekey=None):
            ngr = width // 64
            p.op("vector", lambda E, width=width, srccol=srccol: E.tensor_copy(
                bfs[:, 0:width], stg2[:, srccol:srccol + width]), "VEC")
            if freekey is not None:
                p.gset(freekey, "VEC")
            wb3 = bfs[:, 0:width].rearrange("q (g k) -> q g k", k=64)
            p.op("vector", lambda E, wb3=wb3, ngr=ngr: E.tensor_reduce(
                stg[:, 3000:3000 + ngr], wb3, AX, ALU.add, apply_absolute_value=True), "VEC")
            VF()
            p.op("vector", lambda E, ngr=ngr: E.tensor_single_scalar(
                bfs[:, 7936:7936 + ngr], stg[:, 3000:3000 + ngr], 1.0 / 64.0, ALU.mult), "VEC")
            VF()
            p.op("vector", lambda E, ngr=ngr: E.tensor_copy(stg[:, 3100:3100 + ngr], bfs[:, 7936:7936 + ngr]), "VEC")
            VF()
            p.op("vector", lambda E, ngr=ngr: E.tensor_single_scalar(
                stg[:, 3100:3100 + ngr], stg[:, 3100:3100 + ngr], 1e-8, ALU.max), "VEC")
            VF()
            p.op("vector", lambda E, ngr=ngr: E.reciprocal(stg[:, 3200:3200 + ngr], stg[:, 3100:3100 + ngr]), "VEC")
            VF()
            p.op("vector", lambda E, ngr=ngr: E.tensor_single_scalar(
                stg[:, 3300:3300 + ngr], stg[:, 3100:3100 + ngr], -1.0, ALU.mult), "VEC")
            VF()
            invb = _bc(stg[:, 3200:3200 + ngr], 64)
            ub = bfs[:, 2048:2048 + width].rearrange("q (g k) -> q g k", k=64)
            p.op("vector", lambda E, wb3=wb3, invb=invb, ub=ub: E.tensor_tensor(ub, wb3, invb, ALU.mult), "VEC")
            p.op("vector", lambda E, width=width: E.tensor_single_scalar(
                bfs[:, 4096:4096 + width], bfs[:, 2048:2048 + width], 0.5, ALU.is_gt), "VEC")
            p.op("vector", lambda E, width=width: E.scalar_tensor_tensor(
                bfs[:, 4096:4096 + width], bfs[:, 2048:2048 + width], -0.5, bfs[:, 4096:4096 + width], ALU.is_lt, ALU.subtract), "VEC")
            negs = _bc(stg[:, 3300:3300 + ngr], 64)
            wh = bfs[:, 0:width].rearrange("q (g k) -> q g k", k=64)
            qb = bfs[:, 4096:4096 + width].rearrange("q (g k) -> q g k", k=64)
            return p.op("vector", lambda E, wh=wh, qb=qb, negs=negs: E.tensor_tensor(wh, qb, negs, ALU.mult), "VEC")

        for it in range(32 if PREP_LVL >= 1.2 else 0):
            q = it % 2
            src = _ap(guw, it * 512 * DSL, [[DSL, 128], [128 * DSL, 4], [1, DSL]])
            p.gwait("sync", ("q_stg2", q))
            ldv = p.dma("sync", stg2[:, q * 1024: q * 1024 + 1024], src, "LQ")
            p.wait("vector", "LQ", ldv)
            vq = quant_ops(1024, srccol=q * 1024, freekey=("q_stg2", q))
            p.wait("tensor", "VEC", vq)
            p.wait("tensor", "LD", v_small)
            for db in range(2):
                reg = bfs2[:, 4096 + 512 * db: 4608 + 512 * db]
                for tt in range(4):
                    tr_evac(reg[:, 128 * tt:128 * (tt + 1)] if False else
                            bfs2[:, 4096 + 512 * db + 128 * tt: 4224 + 512 * db + 128 * tt],
                            bfs[:, tt * 256 + db * 128: tt * 256 + db * 128 + 128])
                vsc = p.val("SCL")
                p.wait("sync", "SCL", vsc)
                dst = _ap(cc_qin, OFF_GU + (db * 128) * HF2 + it * 512, [[HF2, 128], [1, 512]])
                p.dma("sync", dst, bfs2[:, 4096 + 512 * db: 4608 + 512 * db], "SCQ")
            p.gset("q_bfs2w", "SCQ")
            p.gwait("vector", "q_bfs2w")

        for it in range(16 if PREP_LVL >= 1.4 else 0):
            q = it % 2
            p.gwait("sync", ("q_stg2", q))
            ldv = p.dma("sync", stg2[:, q * 1024: q * 1024 + 1024], pjw[128 * it:128 * (it + 1), :], "LQ")
            p.wait("vector", "LQ", ldv)
            vq = quant_ops(1024, srccol=q * 1024, freekey=("q_stg2", q))
            p.wait("tensor", "VEC", vq)
            for hb in range(8):
                tr_evac(bfs2[:, 4096 + 128 * (hb % 8): 4224 + 128 * (hb % 8)],
                        bfs[:, hb * 128:(hb + 1) * 128])
                vsc = p.val("SCL")
                p.wait("sync", "SCL", vsc)
                dst = _ap(cc_qin, OFF_PJ + (hb * 128) * D + it * 128, [[D, 128], [1, 128]])
                p.dma("sync", dst, bfs2[:, 4096 + 128 * (hb % 8): 4224 + 128 * (hb % 8)], "SCQ")
            p.gset("q_bfs2w", "SCQ")
            p.gwait("vector", "q_bfs2w")
            p.gwait("scalar", "q_bfs2w")

        p.wait("sync", "VEC", p.val("VEC"))
        for rt in range(2 if PREP_LVL >= 1.6 else 0):
            p.dma("sync", stg[:, 0:4096], wint[128 * rt:128 * (rt + 1), 0:4096], "LQ")
            p.dma("sync", stg2[:, 0:4096], wint[128 * rt:128 * (rt + 1), 4096:8192], "LQ")
            ldv = p.dma("sync", big1[:, 0:320], wint[128 * rt:128 * (rt + 1), 8192:8512], "LQ")
            p.wait("scalar", "LQ", ldv)
            SCP(bf2[:, 0:4096], stg[:, 0:4096])
            SCP(bf2[:, 4096:8192], stg2[:, 0:4096])
            SCP(bf2[:, 8192:8512], big1[:, 0:320])
            vsc = p.val("SCL")
            p.wait("sync", "SCL", vsc)
            dst = _ap(cc_qin, OFF_WI + (rt * 128) * PROJ_B, [[PROJ_B, 128], [1, PROJ_B]])
            vstr = p.dma("sync", dst, bf2[:, 0:8512], "SCQ")
            p.wait("scalar", "SCQ", vstr)
        for rt in range(4 if PREP_LVL >= 1.8 else 0):
            ldv = p.dma("sync", stg[:, 0:2048], wott[128 * rt:128 * (rt + 1), :], "LQ")
            p.wait("scalar", "LQ", ldv)
            SCP(bf2[:, 0:2048], stg[:, 0:2048])
            vsc = p.val("SCL")
            p.wait("sync", "SCL", vsc)
            dst = _ap(cc_qin, OFF_WO + (rt * 128) * D, [[D, 128], [1, D]])
            vstr = p.dma("sync", dst, bf2[:, 0:2048], "SCQ")
            p.wait("scalar", "SCQ", vstr)

        p.wait("gpsimd", "SCQ", p.val("SCQ"))
        if PREP_LVL >= 3:
            v_qag = cc("AllGather", cc_qin, cc_qout)
        else:
            v_qag = 0

        def ln_full(x_ap, out_bf, mcol, vcol, junk):
            ACT(junk, x_ap, AF.Square, accum_out=vcol)
            ACT(junk, x_ap, AF.Identity, accum_out=mcol)
            p.wait("vector", "SCL", p.val("SCL"))
            TS(mcol, mcol, 1.0 / D, ALU.mult)
            TS(vcol, vcol, 1.0 / D, ALU.mult)
            VF()
            STT(vcol, mcol, mcol, vcol, ALU.mult, ALU.subtract)
            VF()
            TS(vcol, vcol, -1.0, ALU.mult)
            p.wait("scalar", "VEC", p.val("VEC"))
            ACT(vcol, vcol, AF.Sqrt, bias=sb_eps[:, 0:1])
            p.wait("vector", "SCL", p.val("SCL"))
            p.op("vector", lambda E: E.reciprocal(vcol, vcol), "VEC")
            VF()
            p.op("vector", lambda E: E.tensor_scalar(out_bf, x_ap, mcol, vcol, ALU.subtract, ALU.mult), "VEC")
            return p.val("VEC")

        MC = stg[:, 4094:4095]
        VC = stg[:, 4095:4096]
        MCs = (stg[:, 4090:4091], stg[:, 4092:4093])
        VCs = (stg[:, 4091:4092], stg[:, 4093:4094])
        JKB = big2[:, 8192:10240]

        def ln_transpose_pass(src_dram, dstT_bf, W, ntiles, colbase):
            for tt in range(ntiles):
                q = tt % 2
                p.gwait("sync", ("lnp_stg2", q))
                p.gwait("sync", "lnp_spill")
                ldv = p.dma("sync", stg2[:, q * 2048: q * 2048 + 2048], src_dram[128 * tt:128 * (tt + 1), :], "LX")
                p.wait("scalar", "LX", ldv)
                p.gwait("vector", ("lnp_bfs", q))
                vln = ln_full(stg2[:, q * 2048: q * 2048 + 2048], bfs[:, q * 2048: q * 2048 + 2048],
                              MCs[q], VCs[q], JKB)
                p.gset(("lnp_stg2", q), "VEC")
                yield tt, vln
                p.wait("tensor", "VEC", vln)
                for dbk in range(16):
                    tr_evac(dstT_bf[:, dbk * W + colbase + tt * 128: dbk * W + colbase + tt * 128 + 128],
                            bfs[:, q * 2048 + 128 * dbk: q * 2048 + 128 * (dbk + 1)], evac="vector")
                p.gset(("lnp_bfs", q), "VEC")

        wrote_out = [False]

        def ckpt(k):
            if PHASES < k:
                raise _StopEmission()

        try:
            ckpt(2)
            XL = 1152
            p.wait("sync", "VEC", p.val("VEC"))
            p.wait("sync", "SCL", p.val("SCL"))
            for tt, vln in ln_transpose_pass(xh, bf1, XL, 9, 0):
                if tt >= 1:
                    qs = tt % 2
                    p.wait("sync", "VEC", vln)
                    p.dma("sync", xres_d[128 * (tt - 1):128 * tt, :], stg2[:, qs * 2048: qs * 2048 + 2048], "SPL")
                    p.gset("lnp_spill", "SPL")
            v_xlnT = p.val("SCL")

            p.wait("vector", "SCL", v_xlnT)
            p.wait("vector", "LD", v_small)
            for dbk in range(16):
                base = dbk * XL
                p.gwait("vector", "p1_conv")
                p.op("vector", lambda E, base=base, dbk=dbk: E.tensor_scalar(
                    stg[:, 0:1024], bf1[:, base + 125: base + 1149], sb_cwa[:, 4 * dbk:4 * dbk + 1], None, ALU.mult), "VEC")
                for k2 in range(1, 4):
                    STT(stg[:, 0:1024], bf1[:, base + 125 + k2: base + 1149 + k2],
                        sb_cwa[:, 4 * dbk + k2:4 * dbk + k2 + 1], stg[:, 0:1024], ALU.mult, ALU.add)
                vv = p.val("VEC")
                p.wait("scalar", "VEC", vv)
                ACT(bf2[:, dbk * 1024:(dbk + 1) * 1024], stg[:, 0:1024], AF.Silu, bias=sb_cba[:, dbk:dbk + 1])
                p.gset("p1_conv", "SCL")
            v_xcT = p.val("SCL")

            p.wait("tensor", "SCL", v_xcT)
            p.wait("tensor", "LD2", v_small2)
            for ch_ in range(2):
                for dbk in range(16):
                    MM(ps_a[:, :], wbuf[:, 160 * dbk:160 * dbk + 128],
                       bf2[:, dbk * 1024 + ch_ * 512: dbk * 1024 + ch_ * 512 + 512], dbk == 0, dbk == 15)
                for dbk in range(16):
                    MM(ps_b[0:32, :], wbuf[:, 160 * dbk + 128:160 * dbk + 160],
                       bf2[:, dbk * 1024 + ch_ * 512: dbk * 1024 + ch_ * 512 + 512], dbk == 0, dbk == 15,
                       sem=("TEN" if dbk == 15 else None))
                vt = p.val("TEN")
                p.wait("scalar", "TEN", vt)
                SCP(bfs2[:, ch_ * 512: ch_ * 512 + 512], ps_a[:, :])
                SCP(bfs2[0:32, 1024 + ch_ * 512: 1536 + ch_ * 512], ps_b[0:32, :])
                vsc = p.val("SCL")
                p.gset("p1_psab", "SCL")
                p.wait("sync", "SCL", vsc)
                p.dma("sync", ag_bcI[:, ch_ * 512:(ch_ + 1) * 512], bfs2[0:32, 1024 + ch_ * 512: 1536 + ch_ * 512], "SCC")
                if ch_ == 0:
                    p.gwait("tensor", "p1_psab")

            p.wait("tensor", "SCL", p.val("SCL"))
            for dbk in range(16):
                for ch_ in range(2):
                    pss = ps_c if ch_ == 0 else ps_d
                    p.gwait("tensor", ("p1_del", ch_))
                    MM(pss[:, :], wbuf[:, 2560 + 128 * dbk:2560 + 128 * (dbk + 1)],
                       bfs2[:, ch_ * 512: ch_ * 512 + 512], True, True, sem="TEN")
                    vt = p.val("TEN")
                    p.wait("scalar", "TEN", vt)
                    ACT(stg[:, 1024 + ch_ * 512: 1536 + ch_ * 512], pss[:, :], AF.Abs, bias=sb_dtb[:, dbk:dbk + 1])
                    ACT(stg[:, 2048 + ch_ * 512: 2560 + ch_ * 512], pss[:, :], AF.Relu, bias=sb_dtb[:, dbk:dbk + 1])
                    vsc0 = p.val("SCL")
                    p.wait("scalar", "SCL", vsc0)
                    ACT(stg[:, 1024 + ch_ * 512: 1536 + ch_ * 512], stg[:, 1024 + ch_ * 512: 1536 + ch_ * 512], AF.Exp, scale=-1.0)
                    vsc1 = p.val("SCL")
                    p.wait("scalar", "SCL", vsc1)
                    ACT(stg[:, 1024 + ch_ * 512: 1536 + ch_ * 512], stg[:, 1024 + ch_ * 512: 1536 + ch_ * 512], AF.Ln, bias=1.0)
                    vsc2 = p.val("SCL")
                    p.wait("vector", "SCL", vsc2)
                    TT(stg[:, ch_ * 512: ch_ * 512 + 512], stg[:, 1024 + ch_ * 512: 1536 + ch_ * 512],
                       stg[:, 2048 + ch_ * 512: 2560 + ch_ * 512], ALU.add)
                    p.gset(("p1_del", ch_), "VEC")
                p.gwait("vector", "p1_dxc")
                TT(bfs[:, 2048:3072], stg[:, 0:1024], bf2[:, dbk * 1024:(dbk + 1) * 1024], ALU.mult)
                vv = p.val("VEC")
                p.wait("sync", "VEC", vv)
                p.dma("sync", a2a_dI[128 * dbk:128 * (dbk + 1), :], stg[:, 0:1024], "SCC")
                p.dma("sync", a2a_xI[128 * dbk:128 * (dbk + 1), :], bfs[:, 2048:3072], "SCC")
                p.gset("p1_dxc", "SCC")
                p.gwait("scalar", "p1_dxc")

            p.wait("gpsimd", "SCC", p.val("SCC"))
            cc("AllToAll", a2a_dI, a2a_dO)
            v_a2ax = cc("AllToAll", a2a_xI, a2a_xO)
            v_agbc = cc("AllGather", ag_bcI, ag_bcO)

            p.wait("sync", "CCS", v_a2ax)
            for half in range(2):
                hb0 = 4 * half
                if half == 1:
                    p.wait("sync", "SCL", p.val("SCL"))
                    p.wait("sync", "VEC", p.val("VEC"))
                for dt_ in range(2):
                    src = _ap(a2a_dO, dt_ * 128 * TPC + hb0 * DSL * TPC, [[TPC, 128], [DSL * TPC, 4], [1, TPC]])
                    p.dma("sync", big1[:, dt_ * 4096: dt_ * 4096 + 4096], src, "LS")
                    srcx = _ap(a2a_xO, dt_ * 128 * TPC + hb0 * DSL * TPC, [[TPC, 128], [DSL * TPC, 4], [1, TPC]])
                    p.dma("sync", bf1[:, 16384 + dt_ * 4096: 20480 + dt_ * 4096], srcx, "LS")
                v_ldh = p.val("LS")
                p.wait("vector", "LS", v_ldh)
                p.op("vector", lambda E: E.memset(big2[:, 8192:16384], 0.0), "VEC")
                p.wait("sync", "CCS", v_agbc)
                for n_ in range(NST):
                    slot = n_ % 2
                    rsem = "RPA" if slot == 0 else "RPB"
                    slB = wbuf[:, 0:4096] if slot == 0 else bf1[:, 24576:28672]
                    slC = wbuf[:, 4096:8192] if slot == 0 else bf1[:, 28672:32768]
                    p.gwait("sync", ("scan_w", slot))
                    srcB = _ap(ag_bcO, hb0 * 32 * TPC + n_ * TPC, [[0, 128], [32 * TPC, 4], [1, TPC]])
                    vB = p.dma("sync", slB, srcB, rsem)
                    srcC = _ap(ag_bcO, hb0 * 32 * TPC + (NST + n_) * TPC, [[0, 128], [32 * TPC, 4], [1, TPC]])
                    vC = p.dma("sync", slC, srcC, rsem)
                    for dt_ in range(2):
                        p.wait("scalar", "LS", v_ldh)
                        p.gwait("scalar", "scan_a")
                        ACT(big2[:, 0:4096], big1[:, dt_ * 4096: dt_ * 4096 + 4096], AF.Exp,
                            scale=sb_alg[:, dt_ * NST + n_: dt_ * NST + n_ + 1])
                        vsa = p.val("SCL")
                        p.wait("vector", rsem, vB)
                        TT(bfs[:, 0:4096], bf1[:, 16384 + dt_ * 4096: 20480 + dt_ * 4096], slB, ALU.mult)
                        p.wait("vector", "SCL", vsa)
                        p.op("vector", lambda E: E.tensor_tensor_scan(
                            big2[:, 4096:8192], big2[:, 0:4096], bfs[:, 0:4096], 0.0, ALU.mult, ALU.add), "VEC")
                        p.gset("scan_a", "VEC")
                        p.wait("vector", rsem, vC)
                        TT(stg[:, 0:4096], big2[:, 4096:8192], slC, ALU.mult)
                        TT(big2[:, 8192 + dt_ * 4096: 12288 + dt_ * 4096],
                           big2[:, 8192 + dt_ * 4096: 12288 + dt_ * 4096], stg[:, 0:4096], ALU.add)
                    p.gset(("scan_w", slot), "VEC")
                VCP(bfs[:, 0:4096], big2[:, 8192:12288])
                VCP(bfs2[:, 0:4096], big2[:, 12288:16384])
                vv = p.val("VEC")
                p.wait("sync", "VEC", vv)
                for dt_ in range(2):
                    dst = _ap(a2a_yI, dt_ * 128 * TPC + hb0 * DSL * TPC, [[TPC, 128], [DSL * TPC, 4], [1, TPC]])
                    p.dma("sync", dst, (bfs if dt_ == 0 else bfs2)[:, 0:4096], "SCC")
                p.gset("scan_y", "SCC")
                if half == 0:
                    p.gwait("vector", "scan_y")
                    p.gwait("sync", "scan_y")

            if DBG:
                p.wait("sync", "VEC", p.val("VEC"))
                p.wait("sync", "SCL", p.val("SCL"))
                p.dma("sync", _ap(dbg_t1, 0, [[4096, 128], [1, 4096]]), bfs[:, 0:4096], "STR")
                p.dma("sync", _ap(dbg_t2, 0, [[16384, 128], [1, 8192]]), big2[:, 0:8192], "STR")
            p.wait("gpsimd", "SCC", p.val("SCC"))
            v_a2ay = cc("AllToAll", a2a_yI, a2a_yO)
            p.wait("sync", "CCS", v_a2ay)

            ldd = p.dma("sync", big2[:, 6144:8192], _ap(dpar, 0, [[0, 128], [1, D]]), "LX")
            p.wait("sync", "SPL", p.val("SPL"))
            for tt in range(8):
                p.gwait("sync", "ap_tile")
                p.dma("sync", stg2[:, 0:2048], xres_d[128 * tt:128 * (tt + 1), :], "LX")
                for dbk in range(16):
                    j, r = (dbk * 128) // DSL, (dbk * 128) % DSL
                    src = _ap(a2a_yO, j * DSL * TPC + r * TPC + tt * 128, [[TPC, 128], [1, 128]])
                    p.dma("sync", bfs[:, 2048 + 128 * dbk: 2176 + 128 * dbk], src, "LX")
                ldv = p.val("LX")
                p.wait("tensor", "LX", ldv)
                p.gwait("tensor", "ap_yT")
                for dbk in range(16):
                    tr_evac(bfs[:, 128 * dbk:128 * (dbk + 1)], bfs[:, 2048 + 128 * dbk: 2176 + 128 * dbk],
                            evac="vector")
                p.gset("ap_yT", "VEC")
                p.wait("scalar", "LX", ldv)
                vln = ln_full(stg2[:, 0:2048], bf2[:, 4096:6144], MC, VC, big2[:, 8192:10240])
                p.wait("vector", "LX", ldv)
                TT(stg2[:, 0:2048], stg2[:, 0:2048], bfs[:, 0:2048], ALU.add)
                TT(big2[:, 10240:12288], bf2[:, 4096:6144], big2[:, 6144:8192], ALU.mult)
                TT(stg2[:, 0:2048], stg2[:, 0:2048], big2[:, 10240:12288], ALU.add)
                vv = p.val("VEC")
                p.wait("sync", "VEC", vv)
                p.dma("sync", xres_d[128 * tt:128 * (tt + 1), :], stg2[:, 0:2048], "SPL")
                p.gset("ap_tile", "SPL")

            ckpt(3)

            def emit_ffn(ph):
                p.wait("sync", "SPL", p.val("SPL"))
                for tt, vln in ln_transpose_pass(xres_d, bf1, 1024, 8, 0):
                    pass
                v_x2 = p.val("SCL")
                p.wait("tensor", "SCL", v_x2)
                p.wait("tensor", "VEC", p.val("VEC"))
                p.wait("sync", "CCS", v_qag)

                GWSL = [(wbuf[:, 0:8192], bf2[:, 0:8192]),
                        (bf2[:, 8192:16384], bf1[:, 16384:24576])]

                pend_sq = [None]

                def flush_sq():
                    if pend_sq[0] is None:
                        return
                    vsq_, hp_ = pend_sq[0]
                    pend_sq[0] = None
                    p.wait("tensor", "SCL", vsq_)
                    MM(ps_e[0:1, 0:512], sb_one[:, :], bfs2[:, 0:512], hp_ == 0, hp_ == 63)
                    MM(ps_f[0:1, 0:512], sb_one[:, :], bfs2[:, 512:1024], hp_ == 0, hp_ == 63,
                       sem=("TEN" if hp_ == 63 else None))
                    p.gset("ffn_hsq", "TEN")

                for hp4 in range(16):
                    slot = hp4 % 2
                    wsem = "LWA" if slot == 0 else "LWB"
                    gslot, uslot = GWSL[slot]
                    p.gwait("sync", ("ffn_w", slot))
                    for j in range(8):
                        srcg = _ap(cc_qout, j * NQ + OFF_GU + hp4 * 512, [[HF2, 128], [128 * HF2, 2], [1, 512]])
                        p.dma("sync", gslot[:, j * 1024: j * 1024 + 1024], srcg, wsem)
                        srcu = _ap(cc_qout, j * NQ + OFF_GU + HF + hp4 * 512, [[HF2, 128], [128 * HF2, 2], [1, 512]])
                        p.dma("sync", uslot[:, j * 1024: j * 1024 + 1024], srcu, wsem)
                    ldv = p.val(wsem)
                    p.wait("tensor", wsem, ldv)
                    for hpi in range(4):
                        hp = hp4 * 4 + hpi
                        gpar = hp % 2
                        pga, pgb = (ps_a, ps_b) if gpar == 0 else (ps_t, ps_u)
                        flush_sq()
                        p.gwait("tensor", ("ffn_psab", gpar))
                        for ch_ in range(2):
                            for dbk in range(16):
                                MM(pga[:, :] if ch_ == 0 else pgb[:, :],
                                   gslot[:, dbk * 512 + hpi * 128: dbk * 512 + hpi * 128 + 128],
                                   bf1[:, dbk * 1024 + ch_ * 512: dbk * 1024 + ch_ * 512 + 512], dbk == 0, dbk == 15,
                                   sem=("TEN" if (ch_ == 1 and dbk == 15) else None))
                        vt = p.val("TEN")
                        p.wait("scalar", "TEN", vt)
                        p.gwait("scalar", "ffn_sg")
                        ACT(bfs[:, 0:512], pga[:, :], AF.Silu)
                        ACT(bfs[:, 512:1024], pgb[:, :], AF.Silu)
                        vsg = p.val("SCL")
                        p.gset(("ffn_psab", gpar), "SCL")
                        p.gwait("tensor", "ffn_pscd")
                        for ch_ in range(2):
                            for dbk in range(16):
                                MM(ps_c[:, :] if ch_ == 0 else ps_d[:, :],
                                   uslot[:, dbk * 512 + hpi * 128: dbk * 512 + hpi * 128 + 128],
                                   bf1[:, dbk * 1024 + ch_ * 512: dbk * 1024 + ch_ * 512 + 512], dbk == 0, dbk == 15,
                                   sem=("TEN" if (ch_ == 1 and dbk == 15) else None))
                        if hpi == 3:
                            p.gset(("ffn_w", slot), "TEN")
                        vt2 = p.val("TEN")
                        p.wait("vector", "TEN", vt2)
                        p.wait("vector", "SCL", vsg)
                        p.gwait("vector", "ffn_h")
                        p.gwait("vector", "ffn_hq2")
                        TT(bfs[:, 1024:1536], bfs[:, 0:512], ps_c[:, :], ALU.mult)
                        TT(bfs[:, 1536:2048], bfs[:, 512:1024], ps_d[:, :], ALU.mult)
                        vh = p.val("VEC")
                        p.gset("ffn_sg", "VEC")
                        p.gset("ffn_pscd", "VEC")
                        p.wait("scalar", "VEC", vh)
                        p.gwait("scalar", "ffn_hsq")
                        ACT(bfs2[:, 0:1024], bfs[:, 1024:2048], AF.Square)
                        p.gset("ffn_hq2", "SCL")
                        vsq = p.val("SCL")
                        pend_sq[0] = (vsq, hp)
                        p.wait("sync", "VEC", vh)
                        p.dma("sync", hn_d[128 * hp:128 * (hp + 1), :], bfs[:, 1024:2048], "SPL")
                        p.gset("ffn_h", "SPL")
                flush_sq()
                v_ss = p.val("TEN")
                p.wait("scalar", "TEN", v_ss)
                ACT(stg[0:1, 0:512], ps_e[0:1, 0:512], AF.Sqrt, bias=sb_eps[0:1, 1:2], scale=sb_eps[0:1, 3:4])
                ACT(stg[0:1, 512:1024], ps_f[0:1, :], AF.Sqrt, bias=sb_eps[0:1, 1:2], scale=sb_eps[0:1, 3:4])
                vsc = p.val("SCL")
                p.wait("vector", "SCL", vsc)
                p.op("vector", lambda E: E.reciprocal(stg[0:1, 0:1024], stg[0:1, 0:1024]), "VEC")
                vv = p.val("VEC")
                p.wait("sync", "VEC", vv)
                p.dma("sync", inv_d[:, :], stg[0:1, 0:1024], "STR")
                p.wait("sync", "STR", p.val("STR"))
                vrp = p.dma("sync", stg2[:, 0:1024], _ap(inv_d, 0, [[0, 128], [1, 1024]]), "RPA")

                p.wait("sync", "SPL", p.val("SPL"))
                PBK = (ps_a, ps_b, ps_c, ps_d)
                for ch_ in range(2):
                    p.gwait("sync", "ffn_hnres")
                    for kq in range(16):
                        src = _ap(hn_d, (kq * 512) * TPC + ch_ * 512, [[TPC, 128], [128 * TPC, 4], [1, 512]])
                        p.dma("sync", bf2[:, kq * 2048: (kq + 1) * 2048], src, "LH")
                    v_hn = p.val("LH")
                    for mg in range(4):
                        for k in range(64):
                            sl = (k // 8) % 2
                            wsem2 = "LWA" if sl == 0 else "LWB"
                            if k % 8 == 0:
                                j = k // 8
                                p.gwait("sync", ("ffn_pw", sl))
                                srcw = _ap(cc_qout, j * NQ + OFF_PJ + mg * 512, [[D, 128], [128 * D, 8], [1, 512]])
                                ldv = p.dma("sync", wbuf[:, sl * 4096: sl * 4096 + 4096], srcw, wsem2)
                                p.wait("tensor", wsem2, ldv)
                            wap = wbuf[:, sl * 4096 + (k % 8) * 512: sl * 4096 + (k % 8) * 512 + 512]
                            if k == 0:
                                p.wait("tensor", "LH", v_hn)
                                for mi in range(4):
                                    p.gwait("tensor", ("ffn_pp", mi))
                            for mi in range(4):
                                MM(PBK[mi][:, :], wap[:, mi * 128: mi * 128 + 128], bf2[:, k * 512: k * 512 + 512],
                                   k == 0, k == 63, sem=("TEN" if mi == 3 else None))
                            if k % 8 == 7:
                                p.gset(("ffn_pw", sl), "TEN")
                        vt = p.val("TEN")
                        p.wait("vector", "TEN", vt)
                        p.wait("vector", "RPA", vrp)
                        for mi in range(4):
                            p.gwait("vector", "ffn_pe")
                            TT(bfs[:, 1024:1536], PBK[mi][:, :], stg2[:, ch_ * 512: ch_ * 512 + 512], ALU.mult)
                            vv = p.val("VEC")
                            p.gset(("ffn_pp", mi), "VEC")
                            p.wait("tensor", "VEC", vv)
                            if mi == 0:
                                p.gwait("scalar", "ffn_pe2")
                            for sub in range(4):
                                tr_evac(bfs2[:, 2048 + 512 * sub + 128 * mi: 2176 + 512 * sub + 128 * mi],
                                        bfs[:, 1024 + 128 * sub: 1152 + 128 * sub])
                            p.gset("ffn_pe", "SCL")
                        vsc = p.val("SCL")
                        p.wait("sync", "SCL", vsc)
                        for sub in range(4):
                            dst = _ap(sub_d, (ch_ * 512 + sub * 128) * D + mg * 512, [[D, 128], [1, 512]])
                            p.dma("sync", dst, bfs2[:, 2048 + 512 * sub: 2560 + 512 * sub], "STR")
                        p.gset("ffn_pe2", "STR")
                    p.gset("ffn_hnres", "TEN")

                p.wait("sync", "STR", p.val("STR"))
                for tt in range(8):
                    p.gwait("sync", "ffn_res")
                    p.dma("sync", stg2[:, 0:2048], xres_d[128 * tt:128 * (tt + 1), :], "LX")
                    ldv = p.dma("sync", bfs[:, 0:2048], sub_d[128 * tt:128 * (tt + 1), :], "LX")
                    p.wait("vector", "LX", ldv)
                    TT(stg2[:, 0:2048], stg2[:, 0:2048], bfs[:, 0:2048], ALU.add)
                    vv = p.val("VEC")
                    p.wait("sync", "VEC", vv)
                    if ph == 1:
                        p.dma("sync", out_x[128 * tt:128 * (tt + 1), :], stg2[:, 0:2048], "STR")
                        p.gset("ffn_res", "STR")
                    else:
                        p.dma("sync", xres_d[128 * tt:128 * (tt + 1), :], stg2[:, 0:2048], "SPL")
                        p.gset("ffn_res", "SPL")

            emit_ffn(0)
            ckpt(4)

            p.wait("sync", "SPL", p.val("SPL"))
            for tt, vln in ln_transpose_pass(xres_d, bf1, 1024, 8, 0):
                pass
            v_x3 = p.val("SCL")
            p.wait("tensor", "SCL", v_x3)
            p.wait("tensor", "VEC", p.val("VEC"))

            for pg4 in range(17):
                slot = pg4 % 2
                wsem = "LWA" if slot == 0 else "LWB"
                wslot = wbuf[:, 0:8192] if slot == 0 else bf1[:, 16384:24576]
                p0g = pg4 * 512
                pcg = 512 if pg4 < 16 else 320
                p.gwait("sync", ("b_w", slot))
                for j in range(8):
                    srcw = _ap(cc_qout, j * NQ + OFF_WI + p0g, [[PROJ_B, 128], [128 * PROJ_B, 2], [1, pcg]])
                    dst = _ap(wslot, j * 1024, [list(wslot.ap[0]), [512, 2], [1, pcg]])
                    p.dma("sync", dst, srcw, wsem)
                ldv = p.val(wsem)
                p.wait("tensor", wsem, ldv)
                for ph_ in range(2):
                  p0 = p0g + ph_ * 256
                  npt = 2 if p0 + 256 <= PROJ_B else 1
                  p.gwait("tensor", "b_ps")
                  for pt in range(npt):
                    mw = 128 if p0 + pt * 128 + 128 <= PROJ_B else 64
                    pss = [ps_a, ps_b] if pt == 0 else [ps_c, ps_d]
                    for ch_ in range(2):
                        for dbk in range(16):
                            MM(pss[ch_][0:mw, :],
                               wslot[:, dbk * 512 + ph_ * 256 + pt * 128: dbk * 512 + ph_ * 256 + pt * 128 + mw],
                               bf1[:, dbk * 1024 + ch_ * 512: dbk * 1024 + ch_ * 512 + 512], dbk == 0, dbk == 15,
                               sem=("TEN" if dbk == 15 else None))
                  if ph_ == 1:
                    p.gset(("b_w", slot), "TEN")
                  vt = p.val("TEN")
                  for pt in range(npt):
                    prow = p0 + pt * 128
                    pss = [ps_a, ps_b] if pt == 0 else [ps_c, ps_d]
                    p.wait("scalar", "TEN", vt)
                    if prow < I_B:
                        p.gwait("scalar", "b_sg")
                        ACT(bfs[:, 0:512], pss[0][:, :], AF.Silu)
                        ACT(bfs[:, 512:1024], pss[1][:, :], AF.Silu)
                        vsg = p.val("SCL")
                        p.wait("tensor", "SCL", vsg)
                        for sub in range(8):
                            tr_evac(bfs2[:, 128 * sub: 128 * (sub + 1)], bfs[:, 128 * sub:128 * (sub + 1)])
                            if sub == 7:
                                p.gset("b_sg", "SCL")
                        vsc = p.val("SCL")
                        p.wait("sync", "SCL", vsc)
                        p.gwait("sync", "b_sg2")
                        dst = _ap(sg_tp, prow, [[I_B, 128], [128 * I_B, 8], [1, 128]])
                        p.dma("sync", dst, bfs2[:, 0:1024], "STR")
                        p.gset("b_sg2", "STR")
                        p.gwait("scalar", "b_sg2")
                    elif prow < I_B + CONV_B:
                        xb_row = prow - I_B
                        cwt = xb_row // 128
                        p.gwait("scalar", "b_pre")
                        SCP(bfs[:, 3:515], pss[0][:, :])
                        SCP(bfs[:, 515:1027], pss[1][:, :])
                        vpre = p.val("SCL")
                        p.wait("vector", "SCL", vpre)
                        VCP(sb_sta[:, cwt * 6:cwt * 6 + 3], bfs[:, 3:6])
                        VCP(sb_sta[:, cwt * 6 + 3:cwt * 6 + 6], bfs[:, 1024:1027])
                        qcv = cwt % 2
                        cacc = stg2[:, qcv * 1024: qcv * 1024 + 1024]
                        p.gwait("vector", ("b_cv", qcv))
                        p.op("vector", lambda E, cwt=cwt, cacc=cacc: E.tensor_scalar(
                            cacc, bfs[:, 0:1024], sb_cwb[:, 4 * cwt:4 * cwt + 1], None, ALU.mult), "VEC")
                        for k2 in range(1, 4):
                            STT(cacc, bfs[:, k2:k2 + 1024], sb_cwb[:, 4 * cwt + k2:4 * cwt + k2 + 1],
                                cacc, ALU.mult, ALU.add)
                        vcv = p.val("VEC")
                        p.gset("b_pre", "VEC")
                        p.wait("scalar", "VEC", vcv)
                        if xb_row < I_B:
                            ACT(bf2[:, cwt * 1024: cwt * 1024 + 1024], cacc, AF.Silu, bias=sb_cbb[:, cwt:cwt + 1])
                        else:
                            which = (xb_row - I_B) // 128
                            ACT(bc_b[:, which * 1024:(which + 1) * 1024], cacc, AF.Silu, bias=sb_cbb[:, cwt:cwt + 1])
                        p.gset(("b_cv", qcv), "SCL")
                    else:
                        for chd in range(2):
                            ACT(stg[0:64, 1024 + chd * 512: 1536 + chd * 512], pss[chd][0:64, :], AF.Abs, bias=sb_dbb[:, :])
                            ACT(stg[0:64, 2048 + chd * 512: 2560 + chd * 512], pss[chd][0:64, :], AF.Relu, bias=sb_dbb[:, :])
                        vsc0 = p.val("SCL")
                        p.wait("scalar", "SCL", vsc0)
                        ACT(stg[0:64, 1024:2048], stg[0:64, 1024:2048], AF.Exp, scale=-1.0)
                        vsc1 = p.val("SCL")
                        p.wait("scalar", "SCL", vsc1)
                        ACT(stg[0:64, 1024:2048], stg[0:64, 1024:2048], AF.Ln, bias=1.0)
                        vsc2 = p.val("SCL")
                        p.wait("vector", "SCL", vsc2)
                        TT(sb_dt[:, 0:1024], stg[0:64, 1024:2048], stg[0:64, 2048:3072], ALU.add)
                  p.gset("b_ps", "SCL")
            v_inproj = p.val("SCL")

            p.wait("vector", "SCL", v_inproj)
            last3 = _ap(sb_sta[:, 0:204], 3, [list(sb_sta[:, 0:204].ap[0]), [6, 34], [1, 3]])
            VCP(bfs[:, 3072:3174], last3)
            vv = p.val("VEC")
            p.wait("sync", "VEC", vv)
            p.dma("sync", ag_hlI[:, :], bfs[:, 3072:3174], "SCC")
            p.wait("gpsimd", "SCC", p.val("SCC"))
            v_aghl = cc("AllGather", ag_hlI, ag_hlO)

            TS(sb_dt[:, :], sb_dt[:, :], 1e6, ALU.min)
            p.op("vector", lambda E: E.tensor_scalar(sb_sf[:, :], sb_dt[:, :], sb_anb[:, :], None, ALU.mult), "VEC")
            p.op("vector", lambda E: E.tensor_tensor_scan(sb_sf[:, :], sb_sf[:, :], sb_sf[:, :], 0.0, ALU.add, ALU.bypass), "VEC")
            TS(sb_ld[:, :], sb_dt[:, :], 1e-38, ALU.max)
            vv = p.val("VEC")
            p.wait("scalar", "VEC", vv)
            ACT(sb_ld[:, :], sb_ld[:, :], AF.Ln)
            v_dtfin = p.val("SCL")

            p.wait("sync", "CCS", v_aghl)
            for j in range(8):
                p.dma("sync", bfs[:, 3072 + 102 * j: 3174 + 102 * j], ag_hlO[128 * j:128 * (j + 1), :], "LX")
            ldv = p.val("LX")
            p.wait("vector", "LX", ldv)
            for j in range(8):
                if j == 0:
                    p.op("vector", lambda E: E.tensor_scalar(sb_hal[:, :], bfs[:, 3072:3174], sb_hse[:, 0:1], None, ALU.mult), "VEC")
                else:
                    STT(sb_hal[:, :], bfs[:, 3072 + 102 * j: 3174 + 102 * j], sb_hse[:, j:j + 1], sb_hal[:, :], ALU.mult, ALU.add)
                VF()
            p204 = list(stg2[:, 0:204].ap[0])
            fixb = lambda off, n3=3: _ap(stg2[:, 0:204], off, [p204, [6, 34], [1, n3]])
            VCP(fixb(0), sb_hal[:, 0:102])
            sta3 = _ap(sb_sta[:, 0:204], 0, [list(sb_sta[:, 0:204].ap[0]), [6, 34], [1, 3]])
            VCP(fixb(3), sta3)
            VF()
            facc = stg2[:, 256:358]
            facc3 = _ap(stg2[:, 256:358], 0, [p204, [3, 34], [1, 3]])
            tmp3 = _ap(stg2[:, 384:486], 0, [p204, [3, 34], [1, 3]])
            pw = list(sb_cwb[:, 0:136].ap[0])
            for k2 in range(4):
                wk = _ap(sb_cwb[:, 0:136], k2, [pw, [4, 34], [0, 3]])
                if k2 == 0:
                    TT(facc3, fixb(0), wk, ALU.mult)
                else:
                    TT(tmp3, fixb(k2), wk, ALU.mult)
                    VF()
                    TT(facc3, facc3, tmp3, ALU.add)
                    VF()
            cbb3 = _ap(sb_cbb[:, 0:34], 0, [list(sb_cbb[:, 0:34].ap[0]), [1, 34], [0, 3]])
            TT(facc3, facc3, cbb3, ALU.add)
            vcv = p.val("VEC")
            p.wait("scalar", "VEC", vcv)
            bf2p = bf2[:, 0:32768]
            out32 = _ap(bf2p, 0, [list(bf2p.ap[0]), [1024, 32], [1, 3]])
            in32 = _ap(stg2[:, 256:352], 0, [p204, [3, 32], [1, 3]])
            ACT(out32, in32, AF.Silu)
            outc = _ap(bc_b[:, 0:2048], 0, [list(bc_b[:, 0:2048].ap[0]), [1024, 2], [1, 3]])
            inc2 = _ap(stg2[:, 352:358], 0, [p204, [3, 2], [1, 3]])
            ACT(outc, inc2, AF.Silu)
            v_fix = p.val("SCL")

            p.wait("vector", "SCL", max(v_dtfin, v_fix))
            HRUN = big1[:, 0:4096]
            HIN = big1[:, 4096:8192]
            XTC = bf1[:, 16384:20480]
            DGA = bf1[:, 20480:28672]
            YCB = bf1[:, 28672:32768]
            HRB = wbuf[:, 0:4096]
            p.op("vector", lambda E: E.memset(HRUN, 0.0), "VEC")

            for c_ in range(NCH):
                c0 = c_ * CH
                p.gwait("vector", "A_srel")
                if c_ == 0:
                    VCP(stg2[0:64, 0:128], sb_sf[:, c0:c0 + CH])
                else:
                    p.op("vector", lambda E, c0=c0: E.tensor_scalar(
                        stg2[0:64, 0:128], sb_sf[:, c0:c0 + CH], sb_sf[:, c0 - 1:c0], None, ALU.subtract), "VEC")
                v_sr = p.val("VEC")
                p.wait("sync", "VEC", v_sr)
                p.dma("sync", sbc_d[:, :], stg2[0:64, 0:128], "STR")
                p.dma("sync", send_d[:, :], stg2[0:64, 127:128], "STR")
                p.wait("sync", "STR", p.val("STR"))
                p.gwait("sync", "A_sbc")
                r1 = p.dma("sync", stg[:, 0:4096], _ap(sbc_d, 0, [[0, 128], [1, 4096]]), "RPA")
                p.gwait("sync", "A_dto")
                r2 = p.dma("sync", sb_dto[:, :], _ap(send_d, 0, [[0, 128], [1, 64]]), "RPB")

                p.wait("tensor", "VEC", v_sr)
                tr_evac(sb_srt[:, :], stg2[0:64, 0:128], fp32=True)
                p.gset("A_srel", "SCL")
                p.op("scalar", lambda E: E.activation(sb_u[:, :], sb_srt[:, :], AF.Exp), "SCL")
                tr_evac(sb_ldT[:, :], sb_ld[:, c0:c0 + CH], fp32=True)
                tr_evac(bfs2[:, 8000:8128], bc_b[:, c0:c0 + CH])
                vsc = p.val("SCL")
                p.wait("vector", "SCL", vsc)
                TT(sb_bD[:, :], sb_ldT[:, :], sb_srt[:, :], ALU.subtract)
                p.wait("vector", "RPB", r2)
                TT(sb_wT[:, :], sb_dto[:, :], sb_srt[:, :], ALU.subtract)
                VF()
                TT(sb_wT[:, :], sb_wT[:, :], sb_ldT[:, :], ALU.add)
                v_bD = p.val("VEC")
                p.wait("scalar", "VEC", v_bD)
                ACT(sb_dto[:, :], sb_dto[:, :], AF.Exp)
                ACT(sb_wT[:, :], sb_wT[:, :], AF.Exp)
                v_w0 = p.val("SCL")
                p.wait("vector", "SCL", v_w0)
                v_dto = v_w0
                v_w = v_w0

                p.gwait("tensor", "A_xt")
                for xi in range(32):
                    tr_evac(XTC[:, xi * 128:(xi + 1) * 128], bf2[:, xi * 1024 + c0: xi * 1024 + c0 + CH])
                v_xt = p.val("SCL")
                p.gset("A_xt", "SCL")

                p.gwait("tensor", "A_gm")
                p.gwait("tensor", ("A_psb", 1))
                p.op("tensor", lambda E, c0=c0: E.matmul(ps_f[:, 0:128], bc_b[:, c0:c0 + CH],
                                                         bc_b[:, 1024 + c0: 1024 + c0 + CH], start=True, stop=True), "TEN", 1)
                vt = p.val("TEN")
                p.wait("vector", "TEN", vt)
                TT(stg2[:, 128:256], ps_f[:, 0:128], sb_tri[:, :], ALU.mult)
                v_gm = p.val("VEC")
                p.gset("A_gm", "VEC")

                p.wait("vector", "RPA", r1)
                p.gwait("vector", "A_dg0")
                p.wait("vector", "VEC", v_gm)
                gm3 = _ap(stg2[:, 128:256], 0, [list(stg2[:, 128:256].ap[0]), [0, 8], [1, 128]])
                for h0 in range(0, H_B, 8):
                    if h0 == 32:
                        p.wait("sync", "VEC", p.val("VEC"))
                        r1b = p.dma("sync", stg[:, 0:4096], _ap(sbc_d, 32 * 128, [[0, 128], [1, 4096]]), "RPA")
                        p.wait("vector", "RPA", r1b)
                    hloc = h0 % 32
                    mq = stg2[:, 2512:3536]
                    bD3 = _bc(sb_bD[:, h0:h0 + 8], 128)
                    srel8 = stg[:, hloc * 128: hloc * 128 + 1024]
                    TT(stg2[:, 512:1536], srel8, bD3, ALU.add)
                    VF()
                    TS(mq, stg2[:, 512:1536], 80.0, ALU.min)
                    vpe = p.val("VEC")
                    p.wait("scalar", "VEC", vpe)
                    ACT(mq, mq, AF.Exp)
                    vde = p.val("SCL")
                    p.wait("vector", "SCL", vde)
                    TT(DGA[:, h0 * 128:(h0 + 8) * 128], mq, gm3, ALU.mult)
                v_dg = p.val("VEC")
                p.gset("A_dg0", "VEC")
                p.gset("A_sbc", "VEC")

                p.wait("vector", "SCL", p.val("SCL"))
                VCP(HRB, HRUN)
                v_hrb = p.val("VEC")

                p.wait("tensor", "SCL", v_xt)
                p.wait("tensor", "VEC", max(v_dg, v_hrb))
                for blk in range(8):
                    par = blk % 2
                    psa = ps_a if par == 0 else ps_d
                    psb = ps_b if par == 0 else ps_f
                    p.gwait("tensor", ("A_psa", par))
                    for hj in range(8):
                        hh = blk * 8 + hj
                        MM(psa[:, hj * 64:(hj + 1) * 64], DGA[:, hh * 128:(hh + 1) * 128],
                           XTC[:, hh * 64:(hh + 1) * 64], True, True)
                    p.gwait("tensor", ("A_psb", par))
                    MM(psb[:, :], bc_b[:, 1024 + c0: 1024 + c0 + CH],
                       HRB[:, blk * 512:(blk + 1) * 512], True, True, sem="TEN")
                    vt = p.val("TEN")
                    p.wait("vector", "TEN", vt)
                    p.wait("vector", "SCL", v_w)
                    p.gwait("vector", "A_ep")
                    ub = _bc(sb_u[:, blk * 8: blk * 8 + 8], 64)
                    TT(stg2[:, 512:1024], psb[:, :], ub, ALU.mult)
                    p.gset(("A_psb", par), "VEC")
                    TT(stg2[:, 1024:1536], XTC[:, blk * 512:(blk + 1) * 512], sb_dxb[:, blk * 512:(blk + 1) * 512], ALU.mult)
                    TT(stg2[:, 512:1024], stg2[:, 512:1024], stg2[:, 1024:1536], ALU.add)
                    TT(YCB[:, blk * 512:(blk + 1) * 512], stg2[:, 512:1024], psa[:, :], ALU.add)
                    p.gset(("A_psa", par), "VEC")
                    p.gset("A_ep", "VEC")
                    p.gwait("vector", "A_bw8")
                    b3 = _ap(bfs2[:, 8000:8128], 0, [list(bfs2[:, 8000:8128].ap[0]), [0, 8], [1, 128]])
                    w3 = _bc(sb_wT[:, blk * 8: blk * 8 + 8], 128)
                    TT(bfs2[:, 4000:5024], b3, w3, ALU.mult)
                    vv = p.val("VEC")
                    p.wait("tensor", "VEC", vv)
                    for hj in range(8):
                        hh = blk * 8 + hj
                        MM(ps_c[:, hj * 64:(hj + 1) * 64], bfs2[:, 4000 + 128 * hj: 4128 + 128 * hj],
                           XTC[:, hh * 64:(hh + 1) * 64], True, True, sem=("TEN" if hj == 7 else None))
                    p.gset("A_bw8", "TEN")
                    vt2 = p.val("TEN")
                    p.wait("vector", "TEN", vt2)
                    dtb = _bc(sb_dto[:, blk * 8: blk * 8 + 8], 64)
                    TT(HRUN[:, blk * 512:(blk + 1) * 512], HRUN[:, blk * 512:(blk + 1) * 512], dtb, ALU.mult)
                    TT(HRUN[:, blk * 512:(blk + 1) * 512], HRUN[:, blk * 512:(blk + 1) * 512], ps_c[:, :], ALU.add)
                    p.gset("A_dto", "VEC")
                v_ych = p.val("VEC")
                p.wait("sync", "VEC", v_ych)
                if DBG and c_ == 0:
                    p.wait("sync", "SCL", p.val("SCL"))
                    p.dma("sync", _ap(dbg_t2, 0, [[16384, 128], [1, 64]]), sb_u[:, :], "STR")
                    p.dma("sync", _ap(dbg_t2, 64, [[16384, 128], [1, 64]]), sb_wT[:, :], "STR")
                    p.dma("sync", _ap(dbg_t2, 128, [[16384, 128], [1, 64]]), sb_dto[:, :], "STR")
                    p.dma("sync", _ap(dbg_t2, 192, [[16384, 128], [1, 64]]), sb_bD[:, :], "STR")
                    p.dma("sync", _ap(dbg_t2, 256, [[16384, 128], [1, 64]]), sb_ldT[:, :], "STR")
                    p.dma("sync", _ap(dbg_t2, 320, [[16384, 128], [1, 64]]), sb_srt[:, :], "STR")
                    p.dma("sync", _ap(dbg_t2, 384, [[16384, 128], [1, 128]]), stg2[:, 128:256], "STR")
                    p.dma("sync", _ap(dbg_t2, 512, [[16384, 128], [1, 4096]]), big1[:, 0:4096], "STR")
                    p.dma("sync", _ap(dbg_t1, 0, [[4096, 128], [1, 4096]]), DGA[:, 0:4096], "STR")
                p.dma("sync", y3_sp[c0:c0 + CH, :], YCB[:, 0:4096], "STR")
                p.gset("A_ycb", "STR")
                p.gwait("vector", "A_ycb")

            vv = p.val("VEC")
            p.wait("sync", "VEC", vv)
            p.dma("sync", ag_hI[:, :], HRUN, "SCC")
            VCP(stg2[0:64, 0:1], sb_sf[:, TPC - 1:TPC])
            vv = p.val("VEC")
            p.wait("sync", "VEC", vv)
            p.dma("sync", ag_sI[:, :], stg2[0:64, 0:1], "SCC")
            p.wait("gpsimd", "SCC", p.val("SCC"))
            cc("AllGather", ag_hI, ag_hO)
            v_agh = cc("AllGather", ag_sI, ag_sO)
            p.wait("sync", "CCS", v_agh)
            ldv = p.dma("sync", stg2[0:8, 0:64], _ap(ag_sO, 0, [[64, 8], [1, 64]]), "LX")
            p.wait("vector", "LX", ldv)
            VCP(bfs2[0:8, 200:264], stg2[0:8, 0:64])
            vv = p.val("VEC")
            p.wait("tensor", "VEC", vv)
            p.gwait("tensor", "trbank_f")
            p.op("tensor", lambda E: E.matmul(ps_e[0:8, 0:64], sb_sel[:, :], bfs2[0:8, 200:264],
                                              start=True, stop=True), "TEN", 1)
            vt = p.val("TEN")
            p.wait("scalar", "TEN", vt)
            ACT(stg2[0:8, 0:64], ps_e[0:8, 0:64], AF.Exp, bias=sb_seb[:, :])
            p.gset("trbank_f", "SCL")
            vsc = p.val("SCL")
            p.wait("sync", "SCL", vsc)
            p.dma("sync", w8_d[:, :], stg2[0:8, 0:64], "STR")
            p.wait("sync", "STR", p.val("STR"))
            r8 = p.dma("sync", sb_w8[:, :], _ap(w8_d, 0, [[0, 128], [1, 512]]), "RPB")
            p.op("vector", lambda E: E.memset(HIN, 0.0), "VEC")
            p.wait("vector", "RPB", r8)
            for j in range(8):
                hsem = "LWA" if j % 2 == 0 else "LWB"
                p.gwait("sync", ("hc", j % 2))
                buf = stg[:, 0:4096] if j % 2 == 0 else stg2[:, 0:4096]
                ldv = p.dma("sync", buf, ag_hO[128 * j:128 * (j + 1), :], hsem)
                p.wait("vector", hsem, ldv)
                wj = _bc(sb_w8[:, j * 64:(j + 1) * 64], 64)
                TT(buf, buf, wj, ALU.mult)
                TT(HIN, HIN, buf, ALU.add)
                p.gset(("hc", j % 2), "VEC")
            HINB = wbuf[:, 4096:8192]
            VCP(HINB, HIN)
            v_hin = p.val("VEC")
            p.wait("sync", "VEC", v_hin)
            rnw = p.dma("gpsimd", sb_dxb[:, :], _ap(nrmw, 0, [[0, 128], [1, I_B]]), "LD2")
            p.wait("vector", "LD2", rnw)

            for c_ in range(NCH):
                c0 = c_ * CH
                p.gwait("sync", "B_y")
                ldy = p.dma("sync", bf1[:, 16384:20480], y3_sp[c0:c0 + CH, :], "LH")
                lds = p.dma("sync", bf1[:, 20480:24576], sg_tp[c0:c0 + CH, :], "LH")
                tr_evac(sb_uf[:, 0:64], sb_sf[:, c0:c0 + CH], fp32=True)
                p.op("scalar", lambda E: E.activation(sb_uf[:, :], sb_uf[:, :], AF.Exp), "SCL")
                v_uf = p.val("SCL")
                p.wait("tensor", "VEC", v_hin)
                for blk in range(8):
                    par = blk % 2
                    psa = ps_a if par == 0 else ps_b
                    p.gwait("tensor", ("B_psa", par))
                    MM(psa[:, :], bc_b[:, 1024 + c0: 1024 + c0 + CH],
                       HINB[:, blk * 512:(blk + 1) * 512], True, True, sem="TEN")
                    vt = p.val("TEN")
                    p.wait("vector", "TEN", vt)
                    p.wait("vector", "SCL", v_uf)
                    p.wait("vector", "LH", lds)
                    ufb = _bc(sb_uf[:, blk * 8: blk * 8 + 8], 64)
                    TT(stg2[:, 0:512], psa[:, :], ufb, ALU.mult)
                    p.gset(("B_psa", par), "VEC")
                    TT(stg2[:, 512:1024], stg2[:, 0:512], bf1[:, 16384 + blk * 512: 16896 + blk * 512], ALU.add)
                    TT(big1[:, blk * 512:(blk + 1) * 512], stg2[:, 512:1024], bf1[:, 20480 + blk * 512: 20992 + blk * 512], ALU.mult)
                v_yg = p.val("VEC")
                p.gset("B_y", "VEC")
                p.wait("scalar", "VEC", v_yg)
                for blk in range(8):
                    ACT(stg2[:, 1024:1536], big1[:, blk * 512:(blk + 1) * 512], AF.Square,
                        accum_out=stg2[:, 3500 + blk:3501 + blk])
                vsq = p.val("SCL")
                p.wait("vector", "SCL", vsq)
                p.op("vector", lambda E: E.tensor_reduce(stg2[:, 3520:3521], stg2[:, 3500:3508], AX, ALU.add), "VEC")
                VF()
                TS(stg2[:, 3520:3521], stg2[:, 3520:3521], 1.0 / I_B, ALU.mult)
                vv = p.val("VEC")
                p.wait("scalar", "VEC", vv)
                ACT(stg2[:, 3520:3521], stg2[:, 3520:3521], AF.Sqrt, bias=sb_eps[:, 2:3])
                vsc = p.val("SCL")
                p.wait("vector", "SCL", vsc)
                p.op("vector", lambda E: E.reciprocal(stg2[:, 3520:3521], stg2[:, 3520:3521]), "VEC")
                VF()
                for blk in range(8):
                    STT(bfs[:, 4096 + blk * 512: 4608 + blk * 512], big1[:, blk * 512:(blk + 1) * 512],
                        stg2[:, 3520:3521], sb_dxb[:, blk * 512:(blk + 1) * 512], ALU.mult, ALU.mult)
                v_yn = p.val("VEC")
                p.wait("tensor", "VEC", v_yn)
                p.gwait("tensor", "B_ynT")
                for sub in range(32):
                    tr_evac(bf2[:, sub * 1024 + c0: sub * 1024 + c0 + CH], bfs[:, 4096 + 128 * sub: 4224 + 128 * sub])
                p.gset("B_ynT", "SCL")
                p.gwait("vector", "B_ynT")
            v_ynT = p.val("SCL")

            p.wait("tensor", "SCL", v_ynT)
            PBK2 = (ps_a, ps_b, ps_c, ps_d)
            for mg in range(4):
                for ch_ in range(2):
                    for k in range(32):
                        sl = (k // 4) % 2
                        wsem = "LWA" if sl == 0 else "LWB"
                        if k % 4 == 0:
                            p.gwait("sync", ("o_w", sl))
                            srcw = _ap(cc_qout, (k // 4) * NQ + OFF_WO + mg * 512, [[D, 128], [128 * D, 4], [1, 512]])
                            ldv = p.dma("sync", wbuf[:, sl * 2048: sl * 2048 + 2048], srcw, wsem)
                            p.wait("tensor", wsem, ldv)
                        wap = wbuf[:, sl * 2048 + (k % 4) * 512: sl * 2048 + (k % 4) * 512 + 512]
                        if k == 0:
                            for mi in range(4):
                                p.gwait("tensor", ("o_ps", mi))
                        for mi in range(4):
                            MM(PBK2[mi][:, :], wap[:, mi * 128: mi * 128 + 128],
                               bf2[:, k * 1024 + ch_ * 512: k * 1024 + ch_ * 512 + 512], k == 0, k == 31,
                               sem=("TEN" if mi == 3 else None))
                        if k % 4 == 3:
                            p.gset(("o_w", sl), "TEN")
                    vt = p.val("TEN")
                    p.wait("scalar", "TEN", vt)
                    for mi in range(4):
                        p.gwait("scalar", ("o_e", mi % 2))
                        SCP(bfs[:, (mi % 2) * 1024: (mi % 2) * 1024 + 512], PBK2[mi][:, :])
                        vsc = p.val("SCL")
                        p.gset(("o_ps", mi), "SCL")
                        p.wait("tensor", "SCL", vsc)
                        if mi == 0:
                            p.gwait("scalar", "o_st")
                        for sub in range(4):
                            tr_evac(bfs2[:, 2048 + 512 * sub + 128 * mi: 2176 + 512 * sub + 128 * mi],
                                    bfs[:, (mi % 2) * 1024 + 128 * sub: (mi % 2) * 1024 + 128 * (sub + 1)])
                        p.gset(("o_e", mi % 2), "SCL")
                    vsc = p.val("SCL")
                    p.wait("sync", "SCL", vsc)
                    for sub in range(4):
                        dst = _ap(sub_d, (ch_ * 512 + sub * 128) * D + mg * 512, [[D, 128], [1, 512]])
                        p.dma("sync", dst, bfs2[:, 2048 + 512 * sub: 2560 + 512 * sub], "STR")
                    p.gset("o_st", "STR")

            p.wait("sync", "STR", p.val("STR"))
            for tt in range(8):
                p.gwait("sync", "b_res")
                p.dma("sync", stg2[:, 0:2048], xres_d[128 * tt:128 * (tt + 1), :], "LX")
                ldv = p.dma("sync", bfs[:, 0:2048], sub_d[128 * tt:128 * (tt + 1), :], "LX")
                p.wait("vector", "LX", ldv)
                TT(stg2[:, 0:2048], stg2[:, 0:2048], bfs[:, 0:2048], ALU.add)
                vv = p.val("VEC")
                p.wait("sync", "VEC", vv)
                p.dma("sync", xres_d[128 * tt:128 * (tt + 1), :], stg2[:, 0:2048], "SPL")
                p.gset("b_res", "SPL")

            ckpt(5)

            emit_ffn(1)
            wrote_out[0] = True

        except _StopEmission:
            pass

        if not wrote_out[0]:
            p.wait("sync", "SPL", p.val("SPL"))
            if PHASES < 2:
                p.op("vector", lambda E: E.memset(stg2[:, 0:2048], 0.0), "VEC")
                p.wait("sync", "VEC", p.val("VEC"))
                for tt in range(8):
                    p.dma("sync", out_x[128 * tt:128 * (tt + 1), :], stg2[:, 0:2048], "STR")
            else:
                for tt in range(8):
                    ldv = p.dma("sync", stg2[:, 0:2048], xres_d[128 * tt:128 * (tt + 1), :], "LX")
                    p.wait("sync", "LX", ldv)
                    p.dma("sync", out_x[128 * tt:128 * (tt + 1), :], stg2[:, 0:2048], "STR")

        if DBG:
            p.wait("sync", "CCS", p.val("CCS"))
            p.dma("sync", dbg_qout.ap(), cc_qout.ap(), "STR")
            if PHASES >= 2:
                p.dma("sync", dbg_dO.ap(), a2a_dO.ap(), "STR")
                p.dma("sync", dbg_xO.ap(), a2a_xO.ap(), "STR")
                p.dma("sync", dbg_bcO.ap(), ag_bcO.ap(), "STR")
                p.dma("sync", dbg_yO.ap(), a2a_yO.ap(), "STR")
        p.wait("sync", "STR", p.val("STR"))
        p.emit(block)

    return nc


def stage_inputs(inputs):
    x = np.asarray(inputs["L0_attn_a_x"], np.float32).reshape(B * T, D)
    gu = np.asarray(inputs["L0_ffn_w_gate_up"], np.float32)
    pj = np.asarray(inputs["L0_ffn_w_proj"], np.float32)
    win = np.asarray(inputs["L0_attn_b_W_in"], np.float32)
    wot = np.asarray(inputs["L0_attn_b_W_out"], np.float32)
    winT = np.ascontiguousarray(win.T)
    wotT = np.ascontiguousarray(wot.T)
    xpjT = np.ascontiguousarray(np.asarray(inputs["L0_attn_a_xproj_w"], np.float32).T)
    dtpT = np.ascontiguousarray(np.asarray(inputs["L0_attn_a_dtproj_w"], np.float32).T)
    cwa = np.ascontiguousarray(np.asarray(inputs["L0_attn_a_conv_w"], np.float32)[:, 0, :])
    cba = np.asarray(inputs["L0_attn_a_conv_b"], np.float32).reshape(D, 1)
    dtba = np.asarray(inputs["L0_attn_a_dtproj_b"], np.float32).reshape(D, 1)
    alga = np.asarray(inputs["L0_attn_a_A_log"], np.float32)
    dpar = np.asarray(inputs["L0_attn_a_D_param"], np.float32).reshape(1, D)
    cwb = np.asarray(inputs["L0_attn_b_conv_weight"], np.float32)
    cbb = np.asarray(inputs["L0_attn_b_conv_bias"], np.float32).reshape(CONV_B, 1)
    dtbb = np.asarray(inputs["L0_attn_b_dt_bias"], np.float32).reshape(H_B, 1)
    algb = np.asarray(inputs["L0_attn_b_A_log"], np.float32).reshape(H_B, 1)
    dexp = np.repeat(np.asarray(inputs["L0_attn_b_D"], np.float32), 64).reshape(1, I_B)
    nrmw = np.asarray(inputs["L0_attn_b_norm_weight"], np.float32).reshape(1, I_B)
    idb = np.eye(128, dtype=ml_dtypes.bfloat16)
    idf = np.eye(128, dtype=np.float32)
    tri = np.ascontiguousarray(np.triu(np.ones((128, 128), np.float32)).astype(ml_dtypes.bfloat16))
    maps = []
    for c in range(NCORES):
        t0 = c * TPC
        halo = np.zeros((128, D), np.float32) if c % 4 == 0 else x[t0 - 128:t0]
        xhc = np.concatenate([halo, x[t0:t0 + TPC]], axis=0)
        sel = np.zeros((8, 8), ml_dtypes.bfloat16)
        bias = np.full((8, 1), -1e30, np.float32)
        g0 = (c // 4) * 4
        for j in range(g0, c):
            bias[j, 0] = 0.0
            for k in range(j + 1, c):
                sel[k, j] = 1.0
        hs = np.zeros((1, 8), np.float32)
        if c % 4 != 0:
            hs[0, c - 1] = 1.0
        maps.append(dict(
            xh=np.ascontiguousarray(xhc),
            guw=np.ascontiguousarray(gu[:, c * DSL:(c + 1) * DSL]),
            pjw=np.ascontiguousarray(pj[:, c * 1024:(c + 1) * 1024]),
            wint=np.ascontiguousarray(winT[c * DSL:(c + 1) * DSL, :]),
            wott=np.ascontiguousarray(wotT[c * 512:(c + 1) * 512, :]),
            xpjt=xpjT, dtpt=dtpT, cwa=cwa, cba=cba, dtba=dtba,
            alga=np.ascontiguousarray(alga[c * DSL:(c + 1) * DSL, :]),
            dpar=dpar, cwb=cwb, cbb=cbb, dtbb=dtbb, algb=algb,
            dexp=dexp, nrmw=nrmw, identb=idb, identf=idf, tri01=tri,
            selm=sel, selb=bias, hsel=hs,
            epsc=np.array([[EPS_LN, EPS_FFN, EPS_B, 1.0 / HF, 0, 0, 0, 0]], np.float32),
        ))
    return maps


def kernel(**inputs):
    maps = stage_inputs(inputs)
    with allow_ici_all_to_all():
        nc = build()
        res = run_bass_kernel_spmd(nc, maps, core_ids=list(range(NCORES)))
    out = np.concatenate([np.asarray(res.results[c]["out"], np.float32) for c in range(NCORES)], axis=0)
    return out.reshape(B, T, D)
